# version 21
# baseline (speedup 1.0000x reference)
import numpy as np
from contextlib import ExitStack

import concourse.bass as bass
import concourse.mybir as mybir
from concourse.bass_utils import run_bass_kernel_spmd

F32 = mybir.dt.float32
AF = mybir.ActivationFunctionType
OP = mybir.AluOpType
AX = mybir.AxisListType.X

L = 2048
DMODEL = 1024
DIN = 2048
NSTATE = 16
T = 128
NT = L // T
DC = DIN // 128
DM = DMODEL // 128
DG = 4
NDG = DC // DG
HN = 4
NGH = NSTATE // HN
SW = DG * HN * T
LN_EPS = 1e-5


class Sem:
    def __init__(self, h):
        self.h = h
        self.n = 0


class Prog:
    def __init__(self):
        self.steps = []

    def step(self, engine, fn, waits=(), inc=None):
        for s, v in waits:
            assert v <= s.n, f"wait on future sem value {v} > {s.n}"
        self.steps.append((engine, fn, list(waits), inc))
        if inc is not None:
            inc[0].n += inc[1]

    def emit(self, engine_name, eng):
        last_wait = {}
        for e, fn, waits, inc in self.steps:
            if e != engine_name:
                continue
            for s, v in waits:
                if last_wait.get(id(s), -1) >= v:
                    continue
                eng.wait_ge(s.h, v)
                last_wait[id(s)] = v
            inst = fn(eng)
            if inc is not None:
                assert inst is not None
                inst.then_inc(inc[0].h, inc[1])


def build_program():
    nc = bass.Bass("TRN2", target_bir_lowering=False, debug=False,
                   detect_race_conditions=False)

    d_x = nc.dram_tensor("x", [L, DMODEL], F32, kind="ExternalInput").ap()
    d_winT = nc.dram_tensor("winT", [DMODEL, 2 * DIN], F32, kind="ExternalInput").ap()
    d_woutT = nc.dram_tensor("woutT", [DIN, DMODEL], F32, kind="ExternalInput").ap()
    d_ffnT = nc.dram_tensor("ffnT", [DMODEL, DMODEL], F32, kind="ExternalInput").ap()
    d_disha = nc.dram_tensor("disha", [64, DMODEL], F32, kind="ExternalInput").ap()
    d_wx = nc.dram_tensor("wx", [128, DC * 33], F32, kind="ExternalInput").ap()
    d_chan = nc.dram_tensor("chan", [128, 176], F32, kind="ExternalInput").ap()
    d_cons = nc.dram_tensor("cons", [128, 195], F32, kind="ExternalInput").ap()
    d_out = nc.dram_tensor("out", [L, DMODEL], F32, kind="ExternalOutput").ap()
    d_scr33 = nc.dram_tensor("scr33", [2, 33, T], F32, kind="Internal").ap()
    d_scr2 = nc.dram_tensor("scr2", [2, T], F32, kind="Internal").ap()

    p = Prog()

    with ExitStack() as ctx:
        sb = lambda name, shape: ctx.enter_context(nc.sbuf_tensor(name, shape, F32)).ap()
        ps = lambda name, shape: ctx.enter_context(nc.psum_tensor(name, shape, F32)).ap()

        wbuf = [sb(f"wbuf{i}", [128, 1024]) for i in range(3)]
        NWB = len(wbuf)
        wx_sb = sb("wx_sb", [128, DC * 33])
        disha_sb = sb("disha_sb", [64, DMODEL])
        chan_sb = sb("chan_sb", [128, 176])
        cons_sb = sb("cons_sb", [128, 195])
        a_tmp = sb("a_tmp", [128, NSTATE])
        a_neg = sb("a_neg", [128, NSTATE])

        xstage = [sb(f"xstage{i}", [128, 1024]) for i in range(3)]
        xn_tok = sb("xn_tok", [128, 1024])
        xn_fm = [sb(f"xn_fm{i}", [128, DM * T]) for i in range(2)]
        xc = sb("xc", [128, DC * (T + 3)])
        u_sb = [sb(f"u{i}", [128, DC * T]) for i in range(2)]
        sz_sb = [sb(f"sz{i}", [128, DC * T]) for i in range(2)]
        y_sb = sb("y", [128, DC * T])
        brep = [sb(f"brep{i}", [128, NSTATE * T]) for i in range(2)]
        crep = [sb(f"crep{i}", [128, NSTATE * T]) for i in range(2)]
        dtraw_rep = [sb(f"dtraw{i}", [128, T]) for i in range(2)]
        m2r_rep = sb("m2r", [128, 2 * T])
        rstd2_rep = sb("rstd2", [128, T])
        dt_g = [sb(f"dtg{i}", [128, DG * T]) for i in range(2)]
        dA_pp = [sb(f"dA{i}", [128, SW]) for i in range(2)]
        dBu_pp = [sb(f"dBu{i}", [128, SW]) for i in range(2)]
        h_pp = [sb(f"h{i}", [128, SW]) for i in range(2)]
        part = sb("part", [128, 3 * DG * T])
        c32 = sb("c32", [128, DG * HN])
        state = sb("state", [128, DC * NSTATE])
        xdbl_sb = sb("xdbl_sb", [33, T])
        pooled_sb = [sb(f"pooled{i}", [64, T]) for i in range(2)]
        sqy = sb("sqy", [128, DG * T])
        fmout = sb("fmout", [128, 1024])
        out_tok = fmout
        PTA, PTB, PTC = 0, DG * T, 2 * DG * T
        convacc = [sb(f"convacc{i}", [128, 2 * T]) for i in range(2)]
        dtmp = sb("dtmp", [128, DG * T])
        dts = sb("dts", [128, DG * T])
        dtu_pp = [sb(f"dtu{i}", [128, DG * T]) for i in range(2)]
        sums = sb("sums", [128, 8])
        mu_tok = sb("mu_tok", [128, 2])
        rstd_tok = sb("rstd_tok", [128, 2])
        spc = sb("spc", [128, 128])
        stat_sb = sb("stat_sb", [33, T])

        ps_xz = ps("ps_xz", [128, 1024])
        ps_out = ps("ps_out", [128, 1024])
        ps_misc = ps("ps_misc", [128, 512])
        ps_stat = ps("ps_stat", [128, 512])
        ps_tr = [ps("ps_tr0", [128, 512]), ps("ps_tr1", [128, 512])]

        s_ld = Sem(ctx.enter_context(nc.semaphore("s_ld")))
        s_st = Sem(ctx.enter_context(nc.semaphore("s_st")))
        s_v = Sem(ctx.enter_context(nc.semaphore("s_v")))
        s_s = Sem(ctx.enter_context(nc.semaphore("s_s")))
        s_t = Sem(ctx.enter_context(nc.semaphore("s_t")))
        s_p = Sem(ctx.enter_context(nc.semaphore("s_p")))
        s_wb = [Sem(ctx.enter_context(nc.semaphore(f"s_wb{i}"))) for i in range(NWB)]
        s_wc = [Sem(ctx.enter_context(nc.semaphore(f"s_wc{i}"))) for i in range(NWB)]
        s_x = Sem(ctx.enter_context(nc.semaphore("s_x")))
        s_scr33 = Sem(ctx.enter_context(nc.semaphore("s_scr33")))
        s_dtr = Sem(ctx.enter_context(nc.semaphore("s_dtr")))
        s_bc = Sem(ctx.enter_context(nc.semaphore("s_bc")))
        s_cc = Sem(ctx.enter_context(nc.semaphore("s_cc")))
        s_sc2 = Sem(ctx.enter_context(nc.semaphore("s_sc2")))
        s_m2r = Sem(ctx.enter_context(nc.semaphore("s_m2r")))

        col_wdt = lambda dc: chan_sb[:, dc:dc + 1]
        col_bdt = lambda dc: chan_sb[:, 16 + dc:16 + dc + 1]
        col_cb = lambda dc: chan_sb[:, 32 + dc:32 + dc + 1]
        col_D = lambda dc: chan_sb[:, 48 + dc:48 + dc + 1]
        col_g = lambda dc: chan_sb[:, 64 + dc:64 + dc + 1]
        col_b = lambda dc: chan_sb[:, 80 + dc:80 + dc + 1]
        col_cw = lambda dc, k: chan_sb[:, 96 + dc * 4 + k:96 + dc * 4 + k + 1]
        alog = chan_sb[:, 160:176]
        mpool = cons_sb[:, 0:64]
        ones_c = cons_sb[:, 64:65]
        ident = cons_sb[:, 65:193]
        eps_col = cons_sb[:, 193:194]
        rden_col = cons_sb[:, 194:195]

        def dma(dst, src):
            return lambda eng: eng.dma_start(out=dst, in_=src)

        def vspace():
            p.step("vector", lambda v: v.tensor_copy(spc[:], cons_sb[:, 65:193]))

        wmark = [0] * NWB
        wfree = [0] * NWB
        tr_last = [("v", 0), ("v", 0)]
        u_free_v = [0, 0]
        sz_free_v = [0, 0]
        xn_free_t = [0, 0]
        x_free_v = [0, 0, 0]
        pool_free_t = [0, 0]
        brep_free_p = [0, 0]
        dtr_free_s = [0, 0]
        dA_free = [0, 0]
        dBu_free = [0, 0]
        crep_free_p = [0, 0]
        y_free_t = 0
        store_done = 0
        xz_evac_mark = 0
        misc_free_s = 0
        stat_free_s = 0
        pool_evac_v = 0
        dtu_vmark = [0, 0]
        xn_tok_free_t = 0
        prev_y3_v = 0
        last_ctr_t = 0
        prev_u_done = [0, 0]

        p.step("sync", dma(wx_sb, d_wx), inc=(s_ld, 16))
        p.step("sync", dma(disha_sb, d_disha), inc=(s_ld, 16))
        p.step("sync", dma(chan_sb, d_chan), inc=(s_ld, 16))
        p.step("sync", dma(cons_sb, d_cons), inc=(s_ld, 16))
        ld_setup = s_ld.n
        p.step("vector", lambda v: v.memset(state[:], 0.0), waits=[(s_ld, ld_setup)])
        p.step("vector", lambda v: v.memset(xc[:], 0.0))
        p.step("vector", lambda v: v.tensor_scalar_min(a_tmp[:], alog, 5.0), inc=(s_v, 1))
        va = s_v.n
        p.step("scalar", lambda s: s.activation(out=a_neg[:], in_=a_tmp[:], func=AF.Exp),
               waits=[(s_v, va)], inc=(s_s, 1))
        sa = s_s.n
        p.step("vector", lambda v: v.tensor_scalar_mul(a_tmp[:], a_neg[:], -1.0),
               waits=[(s_s, sa)], inc=(s_v, 1))


        def phase_A_head(tau):
            nonlocal pool_evac_v, xn_tok_free_t
            par = tau % 2
            t0 = tau * T
            par3 = tau % 3
            st = dict(par=par, misc_free_at_entry=misc_free_s)
            p.step("sync", dma(xstage[par3][:], d_x[t0:t0 + T, :]),
                   waits=[(s_v, x_free_v[par3])], inc=(s_x, 16))
            xw = s_x.n
            for dc in range(DC):
                o = dc * (T + 3)
                p.step("vector",
                       (lambda o=o: lambda v: v.tensor_copy(xc[:, o:o + 3], xc[:, o + T:o + T + 3]))(),
                       inc=((s_v, 1) if dc == DC - 1 else None))
            halo_done = s_v.n
            p.step("vector", (lambda par3=par3: lambda v: v.reduce_sum(sums[:, 0:1], xstage[par3][:], AX))(),
                   waits=[(s_x, xw)])
            p.step("scalar",
                   (lambda par=par, par3=par3: lambda s: s.activation(out=u_sb[par][:, 0:1024], in_=xstage[par3][:], func=AF.Square))(),
                   waits=[(s_x, xw), (s_v, u_free_v[par])], inc=(s_s, 1))
            sqd = s_s.n
            p.step("vector",
                   (lambda par=par: lambda v: v.reduce_sum(sums[:, 1:2], u_sb[par][:, 0:1024], AX))(),
                   waits=[(s_s, sqd)])
            p.step("vector", lambda v: v.tensor_scalar_mul(mu_tok[:, 0:1], sums[:, 0:1], 1.0 / DMODEL))
            vspace()
            p.step("vector", lambda v: v.tensor_tensor(sums[:, 2:3], mu_tok[:, 0:1], mu_tok[:, 0:1], OP.mult))
            p.step("vector", lambda v: v.tensor_scalar_mul(sums[:, 3:4], sums[:, 1:2], 1.0 / DMODEL))
            vspace()
            p.step("vector", lambda v: v.tensor_tensor(sums[:, 3:4], sums[:, 3:4], sums[:, 2:3], OP.subtract),
                   inc=(s_v, 1))
            var_done = s_v.n
            p.step("scalar", lambda s: s.activation(out=sums[:, 2:3], in_=sums[:, 3:4], func=AF.Sqrt, bias=eps_col),
                   waits=[(s_v, var_done)], inc=(s_s, 1))
            sd_done = s_s.n
            p.step("vector", lambda v: v.reciprocal(rstd_tok[:, 0:1], sums[:, 2:3]),
                   waits=[(s_s, sd_done)])
            vspace()
            p.step("vector",
                   (lambda par3=par3: lambda v: v.tensor_scalar(
                       xn_tok[:], xstage[par3][:], mu_tok[:, 0:1], rstd_tok[:, 0:1],
                       OP.subtract, OP.mult))(),
                   waits=[(s_t, xn_tok_free_t)], inc=(s_v, 1))
            xn_done = s_v.n
            for dm in range(DM):
                bi = dm % 2
                te, tm = tr_last[bi]
                tw = [(s_v, xn_done)] + ([(s_s, tm)] if te == "s" else [(s_v, tm)])
                p.step("tensor",
                       (lambda dm=dm, bi=bi: lambda t: t.transpose(
                           ps_tr[bi][:, 0:128], xn_tok[:, dm * 128:(dm + 1) * 128], ident))(),
                       waits=tw, inc=(s_t, 1))
                trd = s_t.n
                p.step("scalar",
                       (lambda dm=dm, bi=bi, par=par: lambda s: s.copy(
                           xn_fm[par][:, dm * T:(dm + 1) * T], ps_tr[bi][:, 0:128]))(),
                       waits=[(s_t, trd), (s_t, xn_free_t[par])], inc=(s_s, 1))
                tr_last[bi] = ("s", s_s.n)
            xn_tok_free_t = s_t.n
            xn_fm_done = s_s.n
            st.update(xw=xw, halo_done=halo_done, xn_fm_done=xn_fm_done)
            return st

        def phase_A_pool(st):
            nonlocal pool_evac_v
            par = st["par"]
            xn_fm_done = st["xn_fm_done"]
            for dm in range(DM):
                p.step("tensor",
                       (lambda dm=dm, par=par: lambda t: t.matmul(
                           ps_stat[0:64, T:2 * T], mpool, xn_fm[par][:, dm * T:(dm + 1) * T],
                           start=(dm == 0), stop=(dm == DM - 1)))(),
                       waits=[(s_s, max(xn_fm_done, stat_free_s))],
                       inc=((s_t, 1) if dm == DM - 1 else None))
            pool_done_t = s_t.n
            p.step("vector",
                   (lambda par=par: lambda v: v.tensor_copy(pooled_sb[par][:], ps_stat[0:64, T:2 * T]))(),
                   waits=[(s_t, pool_done_t), (s_t, pool_free_t[par])], inc=(s_v, 1))
            pool_evac_v = s_v.n

        def phase_A_eg_mm(tau, eg, st):
            nonlocal xz_evac_mark
            par = st["par"]
            xn_fm_done = st["xn_fm_done"]
            for dm in range(DM):
                q = (eg * DM + dm) % NWB
                p.step("sync", (lambda q=q, dm=dm, eg=eg: lambda e: e.dma_start(
                    out=wbuf[q],
                    in_=d_winT[dm * 128:(dm + 1) * 128, eg * 1024:(eg + 1) * 1024]))(),
                       waits=[(s_t, wfree[q])], inc=(s_wb[q], 16))
                wmark[q] = s_wb[q].n
                for e in range(8):
                    p.step("tensor",
                           (lambda q=q, dm=dm, e=e, par=par: lambda t: t.matmul(
                               ps_xz[:, e * T:(e + 1) * T],
                               wbuf[q][:, e * 128:(e + 1) * 128],
                               xn_fm[par][:, dm * T:(dm + 1) * T],
                               start=(dm == 0 and e % 4 == 0),
                               stop=(dm == DM - 1 and e % 4 == 3)))(),
                           waits=[(s_wb[q], wmark[q]), (s_s, max(xn_fm_done, xz_evac_mark))],
                           inc=((s_t, 1) if e == 7 else None))
                wfree[q] = s_t.n
            st[f"mm{eg}"] = s_t.n

        def phase_A_eg_ev(tau, eg, st):
            nonlocal xz_evac_mark, prev_u_done
            par = st["par"]
            halo_done = st["halo_done"]
            misc_free_at_entry = st["misc_free_at_entry"]
            mm_done = st[f"mm{eg}"]
            if True:
                if eg < 2:
                    dc0 = eg * 8
                    xc_out = xc[:].rearrange("p (dc w) -> p dc w", dc=DC)[:, dc0:dc0 + 8, 3:3 + T]
                    p.step("scalar",
                           (lambda xc_out=xc_out: lambda s: s.copy(
                               xc_out, ps_xz[:].rearrange("p (e t) -> p e t", e=8)))(),
                           waits=[(s_t, mm_done), (s_v, halo_done)], inc=(s_s, 1))
                    xz_evac_mark = s_s.n
                    xc_done = s_s.n
                    for dc in range(dc0, dc0 + 8):
                        o = dc * (T + 3)
                        ca = convacc[dc % 2]
                        p.step("vector",
                               (lambda o=o, dc=dc, ca=ca: lambda v: v.tensor_scalar(
                                   ca[:, 0:T], xc[:, o:o + T], col_cw(dc, 0), None, OP.mult))(),
                               waits=[(s_s, max(xc_done, prev_u_done[dc % 2]))])
                        for k in (1, 2, 3):
                            src = ca[:, 0:T] if k % 2 == 1 else ca[:, T:2 * T]
                            dst = ca[:, T:2 * T] if k % 2 == 1 else ca[:, 0:T]
                            p.step("vector",
                                   (lambda o=o, dc=dc, k=k, src=src, dst=dst: lambda v: v.scalar_tensor_tensor(
                                       dst, xc[:, o + k:o + k + T], col_cw(dc, k), src,
                                       OP.mult, OP.add))(),
                                   inc=((s_v, 1) if k == 3 else None))
                        conv_done = s_v.n
                        p.step("scalar",
                               (lambda dc=dc, par=par, ca=ca: lambda s: s.activation(
                                   out=u_sb[par][:, dc * T:(dc + 1) * T], in_=ca[:, T:2 * T],
                                   func=AF.Silu, bias=col_cb(dc)))(),
                               waits=[(s_v, conv_done)], inc=(s_s, 1))
                        u_done = s_s.n
                        prev_u_done[dc % 2] = u_done
                        p.step("tensor",
                               (lambda dc=dc, par=par: lambda t: t.matmul(
                                   ps_misc[0:33, 0:T], wx_sb[:, dc * 33:(dc + 1) * 33],
                                   u_sb[par][:, dc * T:(dc + 1) * T],
                                   start=(dc == 0), stop=(dc == DC - 1)))(),
                               waits=[(s_s, u_done if dc > 0 else max(u_done, misc_free_at_entry))],
                               inc=((s_t, 1) if dc == DC - 1 else None))
                else:
                    dcz0 = (eg - 2) * 8
                    p.step("scalar",
                           (lambda dcz0=dcz0, par=par: lambda s: s.activation(
                               out=sz_sb[par][:, dcz0 * T:(dcz0 + 8) * T], in_=ps_xz[:], func=AF.Silu))(),
                           waits=[(s_t, mm_done), (s_v, sz_free_v[par])], inc=(s_s, 1))
                    xz_evac_mark = s_s.n

        def phase_A_tail(tau, st):
            nonlocal misc_free_s
            par = st["par"]
            xw = st["xw"]
            xdbl_mm_done = s_t.n
            p.step("scalar", lambda s: s.copy(xdbl_sb[:], ps_misc[0:33, 0:T]),
                   waits=[(s_t, xdbl_mm_done)], inc=(s_s, 1))
            xdbl_done = s_s.n
            misc_free_s = xdbl_done
            p.step("gpsimd", (lambda tau=tau: lambda e: e.dma_start(
                out=d_scr33[tau % 2], in_=xdbl_sb[:]))(),
                   waits=[(s_s, xdbl_done)], inc=(s_scr33, 16))
            scr_w = s_scr33.n
            p.step("gpsimd", (lambda par=par, tau=tau: lambda e: e.dma_start(
                out=dtraw_rep[par][:], in_=d_scr33[tau % 2, 0:1, :].broadcast_to([128, T])))(),
                   waits=[(s_scr33, scr_w), (s_s, dtr_free_s[par])], inc=(s_dtr, 16))
            p.step("gpsimd", (lambda par=par, tau=tau: lambda e: e.dma_start(
                out=brep[par][:],
                in_=d_scr33[tau % 2, 1:17, :].rearrange("a b -> (a b)")[None, :].broadcast_to([128, NSTATE * T])))(),
                   waits=[(s_v, brep_free_p[par])], inc=(s_bc, 16))
            p.step("gpsimd", (lambda par=par, tau=tau: lambda e: e.dma_start(
                out=crep[par][:],
                in_=d_scr33[tau % 2, 17:33, :].rearrange("a b -> (a b)")[None, :].broadcast_to([128, NSTATE * T])))(),
                   waits=[(s_v, crep_free_p[par])], inc=(s_cc, 16))
            return dict(dtr=s_dtr.n, bc=s_bc.n, cc=s_cc.n, xw=xw, pool_evac=pool_evac_v)

        def phase_B(tau, marks, inter=None):
            nonlocal prev_y3_v, stat_free_s
            par = tau % 2
            dtrd, bcd, ccd = marks["dtr"], marks["bc"], marks["cc"]
            pend_stats = [None]
            sq_t_mark = 0

            lnd_g = {}

            def emit_dt_s(gg):
                dgi = gg % 2
                dc0 = gg * DG
                for i in range(DG):
                    dc = dc0 + i
                    p.step("scalar",
                           (lambda dc=dc, i=i, par=par: lambda s: s.activation(
                               out=dts[:, i * T:(i + 1) * T], in_=dtraw_rep[par][:], func=AF.Exp,
                               scale=col_wdt(dc), bias=col_bdt(dc)))(),
                           waits=([(s_dtr, dtrd)] if i == 0 else []))
                p.step("scalar", (lambda dgi=dgi: lambda s: s.activation(
                       out=dt_g[dgi][:], in_=dts[:], func=AF.Ln, bias=ones_c))(),
                       waits=[(s_v, dtu_vmark[dgi])], inc=(s_s, 1))
                lnd_g[gg] = s_s.n

            def emit_dt_v(gg):
                dgi = gg % 2
                dc0 = gg * DG
                p.step("vector",
                       (lambda dgi=dgi, dc0=dc0, par=par: lambda v: v.tensor_tensor(
                           dtu_pp[dgi][:], dt_g[dgi][:], u_sb[par][:, dc0 * T:(dc0 + DG) * T], OP.mult))(),
                       waits=[(s_s, lnd_g[gg])], inc=(s_v, 1))
                dtu_vmark[dgi] = s_v.n

            dA_done = {}
            dBu_pd = {}

            def emit_dA(gg, gh):
                buf = (gg * NGH + gh) % 2
                dgi = gg % 2
                for j in range(HN):
                    n = gh * HN + j
                    dA_out = dA_pp[buf][:].rearrange("p (dc n t) -> p n dc t", dc=DG, n=HN)[:, j]
                    p.step("scalar",
                           (lambda dA_out=dA_out, n=n, dgi=dgi: lambda s: s.activation(
                               out=dA_out, in_=dt_g[dgi][:], func=AF.Exp, scale=a_tmp[:, n:n + 1]))(),
                           waits=[(s_v, dA_free[buf])],
                           inc=((s_s, 1) if j == HN - 1 else None))
                dA_done[(gg, gh)] = s_s.n

            def emit_dbu(gg, gh):
                pb = gh % 2
                dgi = gg % 2
                bsl = brep[par][:].rearrange("p (n t) -> p n t", n=NSTATE)[:, gh * HN:(gh + 1) * HN]
                dBu_out = dBu_pp[pb][:].rearrange("p (dc n t) -> p dc n t", dc=DG, n=HN)
                dtu_b = dtu_pp[dgi][:].rearrange("p (dc t) -> p dc t", dc=DG)[:, :, None, :].broadcast_to([128, DG, HN, T])
                bsl_b = bsl[:, None, :, :].broadcast_to([128, DG, HN, T])
                p.step("vector",
                       (lambda dBu_out=dBu_out, dtu_b=dtu_b, bsl_b=bsl_b: lambda v: v.tensor_tensor(
                           dBu_out, dtu_b, bsl_b, OP.mult))(),
                       waits=[(s_bc, bcd)], inc=(s_v, 1))
                dBu_pd[(gg, gh)] = s_v.n

            emit_dt_s(0)
            emit_dt_v(0)
            emit_dA(0, 0)
            emit_dbu(0, 0)
            for g in range(NDG):
                dgi = g % 2
                dc0 = g * DG
                if g + 1 < NDG:
                    emit_dt_s(g + 1)
                pend = [None]

                def emit_tree(ent):
                    buf, pb, gh, cm_done, cm_eng = ent
                    gA = dA_pp[buf][:].rearrange("p (dc n t) -> p dc n t", dc=DG, n=HN)
                    l1 = dBu_pp[pb][:, 0:SW // 2].rearrange("p (dc n t) -> p dc n t", dc=DG, n=HN // 2)
                    slot = PTA if gh % 2 == 0 else PTB
                    tw = [(s_p, cm_done)] if cm_eng == "p" else []
                    p.step("vector",
                           (lambda gA=gA, l1=l1: lambda v: v.tensor_tensor(
                               l1, gA[:, :, 0:HN // 2], gA[:, :, HN // 2:HN], OP.add))(),
                           waits=tw, inc=(s_v, 1))
                    dA_free[buf] = s_v.n
                    p.step("vector",
                           (lambda l1=l1, slot=slot: lambda v: v.tensor_tensor(
                               part[:, slot:slot + DG * T].rearrange("p (dc t) -> p dc t", dc=DG),
                               l1[:, :, 0:1, :].rearrange("p dc n t -> p dc (n t)"),
                               l1[:, :, 1:2, :].rearrange("p dc n t -> p dc (n t)"),
                               OP.add))(),
                           inc=(s_v, 1))
                    dBu_free[pb] = s_v.n
                    if gh == 1:
                        p.step("vector", lambda v: v.tensor_tensor(
                            part[:, PTC:PTC + DG * T], part[:, PTA:PTA + DG * T], part[:, PTB:PTB + DG * T], OP.add))
                    elif gh == 3:
                        p.step("vector", lambda v: v.tensor_tensor(
                            part[:, PTA:PTA + DG * T], part[:, PTA:PTA + DG * T], part[:, PTB:PTB + DG * T], OP.add))

                for gh in range(NGH):
                    buf = (g * NGH + gh) % 2
                    pb = gh % 2
                    sts4 = state[:].rearrange("p (dc n) -> p dc n", dc=DC)[:, dc0:dc0 + DG, gh * HN:(gh + 1) * HN][:, :, :, None]
                    dA_fc = dA_pp[buf][:].rearrange("p (dc n t) -> p dc n t", dc=DG, n=HN)[:, :, :, 0:1]
                    dBu_fc = dBu_pp[pb][:].rearrange("p (dc n t) -> p dc n t", dc=DG, n=HN)[:, :, :, 0:1]
                    c32_4 = c32[:].rearrange("p (dc n) -> p dc n", dc=DG)[:, :, :, None]
                    p.step("vector",
                           (lambda dA_fc=dA_fc, sts4=sts4, c32_4=c32_4: lambda v: v.tensor_tensor(c32_4, dA_fc, sts4, OP.mult))(),
                           waits=[(s_s, dA_done[(g, gh)])])
                    p.step("vector",
                           (lambda buf=buf: lambda v: v.memset(
                               dA_pp[buf][:].rearrange("p (s t) -> p s t", s=DG * HN)[:, :, 0:1], 0.0))())
                    p.step("vector",
                           (lambda pb=pb: lambda v: v.tensor_scalar(
                               dBu_pp[pb][:], dBu_pp[pb][:], -10.0, 10.0, OP.max, OP.min))())
                    p.step("vector",
                           (lambda dBu_fc=dBu_fc, c32_4=c32_4: lambda v: v.tensor_tensor(dBu_fc, dBu_fc, c32_4, OP.add))())
                    vspace()
                    hb = (g * NGH + gh) % 2
                    p.step("vector",
                           (lambda buf=buf, pb=pb, hb=hb: lambda v: v.tensor_tensor_scan(
                               h_pp[hb][:], dA_pp[buf][:], dBu_pp[pb][:], 0.0, OP.mult, OP.add))(),
                           inc=(s_v, 1))
                    scan_done = s_v.n
                    if gh == 0 and g + 1 < NDG:
                        emit_dt_v(g + 1)
                    had_tree = pend[0] is not None
                    if had_tree:
                        emit_tree(pend[0])
                        pend[0] = None
                    nxt = (g, gh + 1) if gh + 1 < NGH else ((g + 1, 0) if g + 1 < NDG else None)
                    if nxt is not None:
                        emit_dA(*nxt)
                        emit_dbu(*nxt)
                    csl = crep[par][:].rearrange("p (n t) -> p n t", n=NSTATE)[:, gh * HN:(gh + 1) * HN]
                    csl_b = csl[:, None, :, :].broadcast_to([128, DG, HN, T])
                    h4 = h_pp[hb][:].rearrange("p (dc n t) -> p dc n t", dc=DG, n=HN)
                    gP = dA_pp[buf][:].rearrange("p (dc n t) -> p dc n t", dc=DG, n=HN)
                    p.step("vector",
                           (lambda gP=gP, h4=h4, csl_b=csl_b: lambda v: v.tensor_tensor(
                               gP, h4, csl_b, OP.mult))(),
                           waits=[(s_cc, ccd)], inc=(s_v, 1))
                    cm_done, cm_eng = s_v.n, "v"
                    h_lc = h_pp[hb][:].rearrange("p (dc n t) -> p dc n t", dc=DG, n=HN)[:, :, :, T - 1:T]
                    p.step("vector",
                           (lambda h_lc=h_lc, sts4=sts4: lambda v: v.tensor_copy(sts4, h_lc))())
                    pend[0] = (buf, pb, gh, cm_done, cm_eng)
                    if gh == 1:
                        if inter is not None:
                            inter(2 * g)
                        if pend_stats[0] is not None:
                            pend_stats[0]()
                            pend_stats[0] = None
                emit_tree(pend[0])
                pend[0] = None
                p.step("vector",
                       (lambda dc0=dc0: lambda v: v.tensor_tensor(
                           y_sb[:, dc0 * T:(dc0 + DG) * T], part[:, PTA:PTA + DG * T], part[:, PTC:PTC + DG * T], OP.add))(),
                       waits=[(s_t, y_free_t)], inc=(s_v, 1))
                y_done = s_v.n
                p.step("scalar",
                       (lambda dc0=dc0: lambda s: s.activation(
                           out=sqy[:], in_=y_sb[:, dc0 * T:(dc0 + DG) * T], func=AF.Square))(),
                       waits=[(s_v, y_done), (s_t, sq_t_mark)], inc=(s_s, 1))
                sqd = s_s.n
                if inter is not None:
                    inter(2 * g + 1)

                def emit_stats(g=g, dc0=dc0, y_done=y_done, sqd=sqd):
                    nonlocal sq_t_mark
                    for i in range(DG):
                        dc = dc0 + i
                        p.step("tensor",
                               (lambda dc=dc: lambda t: t.matmul(
                                   ps_stat[0:1, 0:T], ones_c, y_sb[:, dc * T:(dc + 1) * T],
                                   start=(dc == 0), stop=(dc == DC - 1)))(),
                               waits=[(s_v, max(y_done, pool_evac_v)), (s_s, stat_free_s)])
                        p.step("tensor",
                               (lambda i=i, dc=dc: lambda t: t.matmul(
                                   ps_stat[32:33, 0:T], ones_c, sqy[:, i * T:(i + 1) * T],
                                   start=(dc == 0), stop=(dc == DC - 1)))(),
                               waits=[(s_s, sqd)],
                               inc=((s_t, 1) if i == DG - 1 else None))
                    sq_t_mark = s_t.n

                pend_stats[0] = emit_stats
            if pend_stats[0] is not None:
                pend_stats[0]()
                pend_stats[0] = None
            stats_done_t = s_t.n
            p.step("scalar", lambda s: s.activation(out=stat_sb[0:1, :], in_=ps_stat[0:1, 0:T],
                                                    func=AF.Copy, scale=rden_col[0:1, :]),
                   waits=[(s_t, stats_done_t)], inc=(s_s, 1))
            p.step("scalar", lambda s: s.activation(out=stat_sb[32:33, :], in_=ps_stat[32:33, 0:T],
                                                    func=AF.Copy, scale=rden_col[32:33, :]),
                   inc=(s_s, 1))
            fin_s = s_s.n
            stat_free_s = fin_s
            p.step("gpsimd", dma(d_scr2[0:1, :], stat_sb[0:1, :]), waits=[(s_s, fin_s)], inc=(s_sc2, 16))
            p.step("gpsimd", dma(d_scr2[1:2, :], stat_sb[32:33, :]), inc=(s_sc2, 16))
            sc2w = s_sc2.n
            p.step("gpsimd", lambda e: e.dma_start(
                out=m2r_rep[:], in_=d_scr2[:].rearrange("a b -> (a b)")[None, :].broadcast_to([128, 2 * T])),
                   waits=[(s_sc2, sc2w)], inc=(s_m2r, 16))
            m2rd = s_m2r.n
            p.step("vector", lambda v: v.tensor_tensor(sqy[:, 0:T], m2r_rep[:, 0:T], m2r_rep[:, 0:T], OP.mult),
                   waits=[(s_m2r, m2rd)])
            p.step("vector", lambda v: v.tensor_tensor(sqy[:, T:2 * T], m2r_rep[:, T:2 * T], sqy[:, 0:T], OP.subtract),
                   inc=(s_v, 1))
            var2d = s_v.n
            p.step("scalar", lambda s: s.activation(out=sqy[:, 2 * T:3 * T], in_=sqy[:, T:2 * T],
                                                    func=AF.Sqrt, bias=eps_col),
                   waits=[(s_v, var2d)], inc=(s_s, 1))
            sd2d = s_s.n
            p.step("vector", lambda v: v.reciprocal(rstd2_rep[:], sqy[:, 2 * T:3 * T]),
                   waits=[(s_s, sd2d)])
            vspace()
            mu_b = m2r_rep[:, 0:T][:, None, :].broadcast_to([128, DG, T])
            rs_b = rstd2_rep[:][:, None, :].broadcast_to([128, DG, T])
            for g in range(NDG):
                dc0 = g * DG
                ysl = y_sb[:, dc0 * T:(dc0 + DG) * T]
                p.step("vector",
                       (lambda ysl=ysl, mu_b=mu_b: lambda v: v.tensor_tensor(
                           sqy[:].rearrange("p (dc t) -> p dc t", dc=DG),
                           ysl.rearrange("p (dc t) -> p dc t", dc=DG), mu_b, OP.subtract))())
                p.step("vector",
                       (lambda rs_b=rs_b: lambda v: v.tensor_tensor(
                           dtmp[:].rearrange("p (dc t) -> p dc t", dc=DG),
                           sqy[:].rearrange("p (dc t) -> p dc t", dc=DG), rs_b, OP.mult))())
                for i in range(DG):
                    dc = dc0 + i
                    p.step("vector",
                           (lambda i=i, dc=dc: lambda v: v.tensor_scalar(
                               sqy[:, i * T:(i + 1) * T], dtmp[:, i * T:(i + 1) * T],
                               col_g(dc), col_b(dc), OP.mult, OP.add))())
                    p.step("vector",
                           (lambda i=i, dc=dc, par=par: lambda v: v.scalar_tensor_tensor(
                               dtmp[:, i * T:(i + 1) * T], u_sb[par][:, dc * T:(dc + 1) * T], col_D(dc),
                               sqy[:, i * T:(i + 1) * T], OP.mult, OP.add))())
                p.step("vector",
                       (lambda ysl=ysl, dc0=dc0, par=par: lambda v: v.tensor_tensor(
                           ysl, dtmp[:], sz_sb[par][:, dc0 * T:(dc0 + DG) * T], OP.mult))(),
                       inc=(s_v, 1))
            y3_done = s_v.n
            u_free_v[par] = y3_done
            sz_free_v[par] = y3_done
            brep_free_p[par] = y3_done
            crep_free_p[par] = y3_done
            dtr_free_s[par] = s_s.n
            prev_y3_v = y3_done
            return y3_done

        cout = {}
        cev = {}
        outps_free_v = [0]

        def phase_C_mm(tau, y3_done):
            nonlocal y_free_t
            par = tau % 2
            for dc in range(DC):
                q = dc % NWB
                p.step("gpsimd", (lambda q=q, dc=dc: lambda e: e.dma_start(
                    out=wbuf[q], in_=d_woutT[dc * 128:(dc + 1) * 128, :]))(),
                       waits=[(s_t, wfree[q])], inc=(s_wc[q], 16))
                wcmark = s_wc[q].n
                for m in range(8):
                    p.step("tensor",
                           (lambda q=q, dc=dc, m=m: lambda t: t.matmul(
                               ps_out[:, m * 128:(m + 1) * 128], wbuf[q][:, m * 128:(m + 1) * 128],
                               y_sb[:, dc * T:(dc + 1) * T],
                               start=(dc == 0 and m % 4 == 0), stop=False))(),
                           waits=[(s_wc[q], wcmark), (s_v, max(y3_done, outps_free_v[0]))],
                           inc=((s_t, 1) if m == 7 else None))
                wfree[q] = s_t.n
            for dm in range(DM):
                q = dm % NWB
                p.step("gpsimd", (lambda q=q, dm=dm: lambda e: e.dma_start(
                    out=wbuf[q], in_=d_ffnT[dm * 128:(dm + 1) * 128, :]))(),
                       waits=[(s_t, wfree[q])], inc=(s_wc[q], 16))
                wcmark = s_wc[q].n
                for m in range(8):
                    p.step("tensor",
                           (lambda q=q, dm=dm, m=m, par=par: lambda t: t.matmul(
                               ps_out[:, m * 128:(m + 1) * 128], wbuf[q][:, m * 128:(m + 1) * 128],
                               xn_fm[par][:, dm * T:(dm + 1) * T],
                               start=False, stop=False))(),
                           waits=[(s_wc[q], wcmark)],
                           inc=((s_t, 1) if m == 7 else None))
                wfree[q] = s_t.n
            xn_free_t[par] = s_t.n
            for m in range(8):
                p.step("tensor",
                       (lambda m=m, par=par: lambda t: t.matmul(
                           ps_out[:, m * 128:(m + 1) * 128], disha_sb[:, m * 128:(m + 1) * 128],
                           pooled_sb[par][:],
                           start=False, stop=(m % 4 == 3)))(),
                       inc=((s_t, 1) if m == 7 else None))
            out_mm = s_t.n
            y_free_t = out_mm
            pool_free_t[par] = out_mm
            cout[tau] = out_mm

        def phase_C_evac(tau):
            out_mm = cout.pop(tau)
            p.step("vector", lambda v: v.tensor_copy(fmout[:], ps_out[:]),
                   waits=[(s_t, out_mm)], inc=(s_v, 1))
            cev[tau] = s_v.n
            outps_free_v[0] = cev[tau]

        def phase_C_rest(tau, marks):
            nonlocal store_done
            par = tau % 2
            t0 = tau * T
            ev = cev.pop(tau)
            for m in range(8):
                bi = m % 2
                te, tm = tr_last[bi]
                tw = [(s_v, ev)] + ([(s_s, tm)] if te == "s" else [(s_v, tm)])
                p.step("tensor",
                       (lambda m=m, bi=bi: lambda t: t.transpose(
                           ps_tr[bi][:, 0:128], fmout[:, m * 128:(m + 1) * 128], ident))(),
                       waits=tw, inc=(s_t, 1))
                trd = s_t.n
                p.step("vector",
                       (lambda m=m, bi=bi, par3=tau % 3: lambda v: v.tensor_tensor(
                           out_tok[:, m * 128:(m + 1) * 128], ps_tr[bi][0:128, 0:128],
                           xstage[par3][:, m * 128:(m + 1) * 128], OP.add))(),
                       waits=[(s_t, trd), (s_x, marks["xw"]), (s_st, store_done)],
                       inc=(s_v, 1))
                tr_last[bi] = ("v", s_v.n)
            adds_done = s_v.n
            x_free_v[tau % 3] = adds_done
            p.step("sync", (lambda t0=t0: lambda e: e.dma_start(
                out=d_out[t0:t0 + T, :], in_=out_tok[:]))(),
                   waits=[(s_v, adds_done)], inc=(s_st, 16))
            store_done = s_st.n


        marks = {}
        y3s = {}
        st0 = phase_A_head(0)
        phase_A_pool(st0)
        for e in range(4):
            phase_A_eg_mm(0, e, st0)
            phase_A_eg_ev(0, e, st0)
        marks[0] = phase_A_tail(0, st0)
        for tau in range(NT):
            pieces = []
            if tau >= 1:
                phase_C_mm(tau - 1, y3s[tau - 1])
            if tau + 1 < NT:
                cell = {}
                pieces.append((lambda cell=cell, t2=tau + 1: (lambda: cell.__setitem__("st", phase_A_head(t2))))())
                pieces.append((lambda cell=cell, t2=tau + 1: (lambda: phase_A_eg_mm(t2, 0, cell["st"])))())
            if tau >= 1:
                pieces.append((lambda t1=tau - 1: (lambda: phase_C_evac(t1)))())
            if tau + 1 < NT:
                pieces.append((lambda cell=cell: (lambda: phase_A_pool(cell["st"])))())
                pieces.append((lambda cell=cell, t2=tau + 1: (lambda: phase_A_eg_ev(t2, 0, cell["st"])))())
                pieces.append((lambda cell=cell, t2=tau + 1: (lambda: phase_A_eg_mm(t2, 1, cell["st"])))())
            if tau >= 1:
                pieces.append((lambda t1=tau - 1: (lambda: phase_C_rest(t1, marks[t1])))())
            if tau + 1 < NT:
                for e in range(1, 4):
                    pieces.append((lambda e=e, cell=cell, t2=tau + 1: (lambda: phase_A_eg_ev(t2, e, cell["st"])))())
                    if e + 1 < 4:
                        pieces.append((lambda e=e, cell=cell, t2=tau + 1: (lambda: phase_A_eg_mm(t2, e + 1, cell["st"])))())
                pieces.append((lambda cell=cell, t2=tau + 1: (lambda: marks.__setitem__(t2, phase_A_tail(t2, cell["st"])))) ())

            TAKES = [2, 2, 2, 3, 3, 1, 99, 99]

            def inter(k, pieces=pieces, TAKES=TAKES):
                for _ in range(TAKES[k]):
                    if pieces:
                        pieces.pop(0)()

            y3s[tau] = phase_B(tau, marks[tau], inter)
        phase_C_mm(NT - 1, y3s[NT - 1])
        phase_C_evac(NT - 1)
        phase_C_rest(NT - 1, marks[NT - 1])

        final_store = s_st.n
        p.step("sync", lambda e: e.nop(), waits=[(s_st, final_store)])

        with nc.Block() as block:
            @block.sync
            def _(sync):
                p.emit("sync", sync)

            @block.vector
            def _(vector):
                p.emit("vector", vector)

            @block.scalar
            def _(scalar):
                p.emit("scalar", scalar)

            @block.tensor
            def _(tensor):
                p.emit("tensor", tensor)

            @block.gpsimd
            def _(gpsimd):
                p.emit("gpsimd", gpsimd)

    return nc


def _host_layout(inputs):
    W_in = np.ascontiguousarray(inputs["L0_attn_W_in"])
    W_out = np.ascontiguousarray(inputs["L0_attn_W_out"])
    ffn_w = np.ascontiguousarray(inputs["L0_ffn_weight"])
    winT = np.ascontiguousarray(W_in.T.astype(np.float32))
    woutT = np.ascontiguousarray(W_out.T.astype(np.float32))
    ffnT = np.ascontiguousarray(ffn_w.T.astype(np.float32))
    disha = np.ascontiguousarray(inputs["L0_ffn_disha"].astype(np.float32))

    W_x = inputs["L0_attn_W_x"]
    wx = np.ascontiguousarray(
        W_x.T.reshape(DC, 128, 33).transpose(1, 0, 2).reshape(128, DC * 33).astype(np.float32))

    def chanv(a):
        return a.reshape(DC, 128).T

    chan = np.zeros((128, 176), np.float32)
    chan[:, 0:16] = chanv(inputs["L0_attn_W_dt"][:, 0])
    chan[:, 16:32] = chanv(inputs["L0_attn_b_dt"])
    chan[:, 32:48] = chanv(inputs["L0_attn_conv_b"])
    chan[:, 48:64] = chanv(inputs["L0_attn_D_param"])
    chan[:, 64:80] = chanv(inputs["L0_attn_ln_g"])
    chan[:, 80:96] = chanv(inputs["L0_attn_ln_b"])
    cw = inputs["L0_attn_conv_w"][:, 0, :]
    chan[:, 96:160] = cw.reshape(DC, 128, 4).transpose(1, 0, 2).reshape(128, 64)
    chan[:, 160:176] = np.broadcast_to(inputs["L0_attn_A_log"], (128, 16))

    cons = np.zeros((128, 195), np.float32)
    cons[:, 0:64] = np.tile(np.eye(64, dtype=np.float32), (2, 1))
    cons[:, 64] = 1.0
    cons[:, 65:193] = np.eye(128, dtype=np.float32)
    cons[:, 193] = LN_EPS
    cons[:, 194] = 1.0 / DIN

    return dict(winT=winT, woutT=woutT, ffnT=ffnT, disha=disha, wx=wx,
                chan=chan, cons=cons)


def kernel(**inputs):
    shared = _host_layout(inputs)
    x = np.ascontiguousarray(inputs["L0_attn_x"].astype(np.float32))
    nc = build_program()
    in_maps = []
    for b in range(8):
        m = dict(shared)
        m["x"] = np.ascontiguousarray(x[b])
        in_maps.append(m)
    res = run_bass_kernel_spmd(nc, in_maps, core_ids=list(range(8)))
    out = np.stack([r["out"] for r in res.results], axis=0)
    return out.astype(np.float32)


# version 22
# speedup vs baseline: 1.0268x; 1.0268x over previous
import numpy as np
from contextlib import ExitStack

import concourse.bass as bass
import concourse.mybir as mybir
from concourse.bass_utils import run_bass_kernel_spmd

F32 = mybir.dt.float32
AF = mybir.ActivationFunctionType
OP = mybir.AluOpType
AX = mybir.AxisListType.X

L = 2048
DMODEL = 1024
DIN = 2048
NSTATE = 16
T = 128
NT = L // T
DC = DIN // 128
DM = DMODEL // 128
DG = 4
NDG = DC // DG
HN = 4
NGH = NSTATE // HN
SW = DG * HN * T
LN_EPS = 1e-5


class Sem:
    def __init__(self, h):
        self.h = h
        self.n = 0


class Prog:
    def __init__(self):
        self.steps = []

    def step(self, engine, fn, waits=(), inc=None):
        for s, v in waits:
            assert v <= s.n, f"wait on future sem value {v} > {s.n}"
        self.steps.append((engine, fn, list(waits), inc))
        if inc is not None:
            inc[0].n += inc[1]

    def emit(self, engine_name, eng):
        last_wait = {}
        for e, fn, waits, inc in self.steps:
            if e != engine_name:
                continue
            for s, v in waits:
                if last_wait.get(id(s), -1) >= v:
                    continue
                eng.wait_ge(s.h, v)
                last_wait[id(s)] = v
            inst = fn(eng)
            if inc is not None:
                assert inst is not None
                inst.then_inc(inc[0].h, inc[1])


def build_program():
    nc = bass.Bass("TRN2", target_bir_lowering=False, debug=False,
                   detect_race_conditions=False)

    d_x = nc.dram_tensor("x", [L, DMODEL], F32, kind="ExternalInput").ap()
    d_winT = nc.dram_tensor("winT", [DMODEL, 2 * DIN], F32, kind="ExternalInput").ap()
    d_woutT = nc.dram_tensor("woutT", [DIN, DMODEL], F32, kind="ExternalInput").ap()
    d_ffnT = nc.dram_tensor("ffnT", [DMODEL, DMODEL], F32, kind="ExternalInput").ap()
    d_disha = nc.dram_tensor("disha", [64, DMODEL], F32, kind="ExternalInput").ap()
    d_wx = nc.dram_tensor("wx", [128, DC * 33], F32, kind="ExternalInput").ap()
    d_chan = nc.dram_tensor("chan", [128, 176], F32, kind="ExternalInput").ap()
    d_cons = nc.dram_tensor("cons", [128, 195], F32, kind="ExternalInput").ap()
    d_out = nc.dram_tensor("out", [L, DMODEL], F32, kind="ExternalOutput").ap()
    d_scr33 = nc.dram_tensor("scr33", [2, 33, T], F32, kind="Internal").ap()
    d_scr2 = nc.dram_tensor("scr2", [2, T], F32, kind="Internal").ap()

    p = Prog()

    with ExitStack() as ctx:
        sb = lambda name, shape: ctx.enter_context(nc.sbuf_tensor(name, shape, F32)).ap()
        ps = lambda name, shape: ctx.enter_context(nc.psum_tensor(name, shape, F32)).ap()

        wbuf = [sb(f"wbuf{i}", [128, 1024]) for i in range(3)]
        NWB = len(wbuf)
        wx_sb = sb("wx_sb", [128, DC * 33])
        disha_sb = sb("disha_sb", [64, DMODEL])
        chan_sb = sb("chan_sb", [128, 176])
        cons_sb = sb("cons_sb", [128, 195])
        a_tmp = sb("a_tmp", [128, NSTATE])
        a_neg = sb("a_neg", [128, NSTATE])

        xstage = [sb(f"xstage{i}", [128, 1024]) for i in range(3)]
        xn_tok = sb("xn_tok", [128, 1024])
        xn_fm = [sb(f"xn_fm{i}", [128, DM * T]) for i in range(2)]
        xc = sb("xc", [128, DC * (T + 3)])
        u_sb = [sb(f"u{i}", [128, DC * T]) for i in range(2)]
        sz_sb = [sb(f"sz{i}", [128, DC * T]) for i in range(2)]
        y_sb = sb("y", [128, DC * T])
        brep = [sb(f"brep{i}", [128, NSTATE * T]) for i in range(2)]
        crep = [sb(f"crep{i}", [128, NSTATE * T]) for i in range(2)]
        dtraw_rep = [sb(f"dtraw{i}", [128, T]) for i in range(2)]
        m2r_rep = sb("m2r", [128, 2 * T])
        rstd2_rep = sb("rstd2", [128, T])
        dt_g = [sb(f"dtg{i}", [128, DG * T]) for i in range(2)]
        dA_pp = [sb(f"dA{i}", [128, SW]) for i in range(2)]
        dBu_pp = [sb(f"dBu{i}", [128, SW]) for i in range(2)]
        h_pp = [sb(f"h{i}", [128, SW]) for i in range(2)]
        part = sb("part", [128, 3 * DG * T])
        c32 = sb("c32", [128, DG * HN])
        state = sb("state", [128, DC * NSTATE])
        xdbl_sb = sb("xdbl_sb", [33, T])
        pooled_sb = [sb(f"pooled{i}", [64, T]) for i in range(2)]
        sqy = sb("sqy", [128, DG * T])
        fmout = sb("fmout", [128, 1024])
        out_tok = fmout
        PTA, PTB, PTC = 0, DG * T, 2 * DG * T
        convacc = [sb(f"convacc{i}", [128, 2 * T]) for i in range(2)]
        dtmp = sb("dtmp", [128, DG * T])
        dts = sb("dts", [128, DG * T])
        dtu_pp = [sb(f"dtu{i}", [128, DG * T]) for i in range(2)]
        sums = sb("sums", [128, 8])
        mu_tok = sb("mu_tok", [128, 2])
        rstd_tok = sb("rstd_tok", [128, 2])
        spc = sb("spc", [128, 128])
        stat_sb = sb("stat_sb", [33, T])

        ps_xz = ps("ps_xz", [128, 1024])
        ps_out = ps("ps_out", [128, 1024])
        ps_misc = ps("ps_misc", [128, 512])
        ps_stat = ps("ps_stat", [128, 512])
        ps_tr = [ps("ps_tr0", [128, 512]), ps("ps_tr1", [128, 512])]

        s_ld = Sem(ctx.enter_context(nc.semaphore("s_ld")))
        s_st = Sem(ctx.enter_context(nc.semaphore("s_st")))
        s_v = Sem(ctx.enter_context(nc.semaphore("s_v")))
        s_s = Sem(ctx.enter_context(nc.semaphore("s_s")))
        s_t = Sem(ctx.enter_context(nc.semaphore("s_t")))
        s_p = Sem(ctx.enter_context(nc.semaphore("s_p")))
        s_wb = [Sem(ctx.enter_context(nc.semaphore(f"s_wb{i}"))) for i in range(NWB)]
        s_wc = [Sem(ctx.enter_context(nc.semaphore(f"s_wc{i}"))) for i in range(NWB)]
        s_x = Sem(ctx.enter_context(nc.semaphore("s_x")))
        s_scr33 = Sem(ctx.enter_context(nc.semaphore("s_scr33")))
        s_dtr = Sem(ctx.enter_context(nc.semaphore("s_dtr")))
        s_bc = Sem(ctx.enter_context(nc.semaphore("s_bc")))
        s_cc = Sem(ctx.enter_context(nc.semaphore("s_cc")))
        s_sc2 = Sem(ctx.enter_context(nc.semaphore("s_sc2")))
        s_m2r = Sem(ctx.enter_context(nc.semaphore("s_m2r")))

        col_wdt = lambda dc: chan_sb[:, dc:dc + 1]
        col_bdt = lambda dc: chan_sb[:, 16 + dc:16 + dc + 1]
        col_cb = lambda dc: chan_sb[:, 32 + dc:32 + dc + 1]
        col_D = lambda dc: chan_sb[:, 48 + dc:48 + dc + 1]
        col_g = lambda dc: chan_sb[:, 64 + dc:64 + dc + 1]
        col_b = lambda dc: chan_sb[:, 80 + dc:80 + dc + 1]
        col_cw = lambda dc, k: chan_sb[:, 96 + dc * 4 + k:96 + dc * 4 + k + 1]
        alog = chan_sb[:, 160:176]
        mpool = cons_sb[:, 0:64]
        ones_c = cons_sb[:, 64:65]
        ident = cons_sb[:, 65:193]
        eps_col = cons_sb[:, 193:194]
        rden_col = cons_sb[:, 194:195]

        def dma(dst, src):
            return lambda eng: eng.dma_start(out=dst, in_=src)

        def vspace():
            p.step("vector", lambda v: v.tensor_copy(spc[:], cons_sb[:, 65:193]))

        wmark = [0] * NWB
        wfree = [0] * NWB
        tr_last = [("v", 0), ("v", 0)]
        u_free_v = [0, 0]
        sz_free_v = [0, 0]
        xn_free_t = [0, 0]
        x_free_v = [0, 0, 0]
        pool_free_t = [0, 0]
        brep_free_p = [0, 0]
        dtr_free_s = [0, 0]
        dA_free = [0, 0]
        dBu_free = [0, 0]
        crep_free_p = [0, 0]
        y_free_t = 0
        store_done = 0
        xz_evac_mark = 0
        misc_free_s = 0
        stat_free_s = 0
        pool_evac_v = 0
        dtu_vmark = [0, 0]
        xn_tok_free_t = 0
        prev_y3_v = 0
        last_ctr_t = 0
        prev_u_done = [0, 0]

        p.step("sync", dma(wx_sb, d_wx), inc=(s_ld, 16))
        p.step("sync", dma(disha_sb, d_disha), inc=(s_ld, 16))
        p.step("sync", dma(chan_sb, d_chan), inc=(s_ld, 16))
        p.step("sync", dma(cons_sb, d_cons), inc=(s_ld, 16))
        ld_setup = s_ld.n
        p.step("vector", lambda v: v.memset(state[:], 0.0), waits=[(s_ld, ld_setup)])
        p.step("vector", lambda v: v.memset(xc[:], 0.0))
        p.step("vector", lambda v: v.tensor_scalar_min(a_tmp[:], alog, 5.0), inc=(s_v, 1))
        va = s_v.n
        p.step("scalar", lambda s: s.activation(out=a_neg[:], in_=a_tmp[:], func=AF.Exp),
               waits=[(s_v, va)], inc=(s_s, 1))
        sa = s_s.n
        p.step("vector", lambda v: v.tensor_scalar_mul(a_tmp[:], a_neg[:], -1.0),
               waits=[(s_s, sa)], inc=(s_v, 1))


        def phase_A_head(tau):
            nonlocal pool_evac_v, xn_tok_free_t
            par = tau % 2
            t0 = tau * T
            par3 = tau % 3
            st = dict(par=par, misc_free_at_entry=misc_free_s)
            p.step("sync", dma(xstage[par3][:], d_x[t0:t0 + T, :]),
                   waits=[(s_v, x_free_v[par3])], inc=(s_x, 16))
            xw = s_x.n
            for dc in range(DC):
                o = dc * (T + 3)
                p.step("vector",
                       (lambda o=o: lambda v: v.tensor_copy(xc[:, o:o + 3], xc[:, o + T:o + T + 3]))(),
                       inc=((s_v, 1) if dc == DC - 1 else None))
            halo_done = s_v.n
            p.step("vector", (lambda par3=par3: lambda v: v.reduce_sum(sums[:, 0:1], xstage[par3][:], AX))(),
                   waits=[(s_x, xw)])
            p.step("scalar",
                   (lambda par=par, par3=par3: lambda s: s.activation(out=u_sb[par][:, 0:1024], in_=xstage[par3][:], func=AF.Square))(),
                   waits=[(s_x, xw), (s_v, u_free_v[par])], inc=(s_s, 1))
            sqd = s_s.n
            p.step("vector",
                   (lambda par=par: lambda v: v.reduce_sum(sums[:, 1:2], u_sb[par][:, 0:1024], AX))(),
                   waits=[(s_s, sqd)])
            p.step("vector", lambda v: v.tensor_scalar_mul(mu_tok[:, 0:1], sums[:, 0:1], 1.0 / DMODEL))
            vspace()
            p.step("vector", lambda v: v.tensor_tensor(sums[:, 2:3], mu_tok[:, 0:1], mu_tok[:, 0:1], OP.mult))
            p.step("vector", lambda v: v.tensor_scalar_mul(sums[:, 3:4], sums[:, 1:2], 1.0 / DMODEL))
            vspace()
            p.step("vector", lambda v: v.tensor_tensor(sums[:, 3:4], sums[:, 3:4], sums[:, 2:3], OP.subtract),
                   inc=(s_v, 1))
            var_done = s_v.n
            p.step("scalar", lambda s: s.activation(out=sums[:, 2:3], in_=sums[:, 3:4], func=AF.Sqrt, bias=eps_col),
                   waits=[(s_v, var_done)], inc=(s_s, 1))
            sd_done = s_s.n
            p.step("vector", lambda v: v.reciprocal(rstd_tok[:, 0:1], sums[:, 2:3]),
                   waits=[(s_s, sd_done)])
            vspace()
            p.step("vector",
                   (lambda par3=par3: lambda v: v.tensor_scalar(
                       xn_tok[:], xstage[par3][:], mu_tok[:, 0:1], rstd_tok[:, 0:1],
                       OP.subtract, OP.mult))(),
                   waits=[(s_t, xn_tok_free_t)], inc=(s_v, 1))
            xn_done = s_v.n
            for dm in range(DM):
                bi = dm % 2
                te, tm = tr_last[bi]
                tw = [(s_v, xn_done)] + ([(s_s, tm)] if te == "s" else [(s_v, tm)])
                p.step("tensor",
                       (lambda dm=dm, bi=bi: lambda t: t.transpose(
                           ps_tr[bi][:, 0:128], xn_tok[:, dm * 128:(dm + 1) * 128], ident))(),
                       waits=tw, inc=(s_t, 1))
                trd = s_t.n
                p.step("scalar",
                       (lambda dm=dm, bi=bi, par=par: lambda s: s.copy(
                           xn_fm[par][:, dm * T:(dm + 1) * T], ps_tr[bi][:, 0:128]))(),
                       waits=[(s_t, trd), (s_t, xn_free_t[par])], inc=(s_s, 1))
                tr_last[bi] = ("s", s_s.n)
            xn_tok_free_t = s_t.n
            xn_fm_done = s_s.n
            st.update(xw=xw, halo_done=halo_done, xn_fm_done=xn_fm_done)
            return st

        def phase_A_pool(st):
            nonlocal pool_evac_v
            par = st["par"]
            xn_fm_done = st["xn_fm_done"]
            for dm in range(DM):
                p.step("tensor",
                       (lambda dm=dm, par=par: lambda t: t.matmul(
                           ps_stat[0:64, T:2 * T], mpool, xn_fm[par][:, dm * T:(dm + 1) * T],
                           start=(dm == 0), stop=(dm == DM - 1)))(),
                       waits=[(s_s, max(xn_fm_done, stat_free_s))],
                       inc=((s_t, 1) if dm == DM - 1 else None))
            pool_done_t = s_t.n
            p.step("vector",
                   (lambda par=par: lambda v: v.tensor_copy(pooled_sb[par][:], ps_stat[0:64, T:2 * T]))(),
                   waits=[(s_t, pool_done_t), (s_t, pool_free_t[par])], inc=(s_v, 1))
            pool_evac_v = s_v.n

        def phase_A_eg_mm(tau, eg, st):
            nonlocal xz_evac_mark
            par = st["par"]
            xn_fm_done = st["xn_fm_done"]
            for dm in range(DM):
                q = (eg * DM + dm) % NWB
                p.step("sync", (lambda q=q, dm=dm, eg=eg: lambda e: e.dma_start(
                    out=wbuf[q],
                    in_=d_winT[dm * 128:(dm + 1) * 128, eg * 1024:(eg + 1) * 1024]))(),
                       waits=[(s_t, wfree[q])], inc=(s_wb[q], 16))
                wmark[q] = s_wb[q].n
                for e in range(8):
                    p.step("tensor",
                           (lambda q=q, dm=dm, e=e, par=par: lambda t: t.matmul(
                               ps_xz[:, e * T:(e + 1) * T],
                               wbuf[q][:, e * 128:(e + 1) * 128],
                               xn_fm[par][:, dm * T:(dm + 1) * T],
                               start=(dm == 0 and e % 4 == 0),
                               stop=(dm == DM - 1 and e % 4 == 3)))(),
                           waits=[(s_wb[q], wmark[q]), (s_s, max(xn_fm_done, xz_evac_mark))],
                           inc=((s_t, 1) if e == 7 else None))
                wfree[q] = s_t.n
            st[f"mm{eg}"] = s_t.n

        def phase_A_eg_ev(tau, eg, st):
            nonlocal xz_evac_mark, prev_u_done
            par = st["par"]
            halo_done = st["halo_done"]
            misc_free_at_entry = st["misc_free_at_entry"]
            mm_done = st[f"mm{eg}"]
            if True:
                if eg < 2:
                    dc0 = eg * 8
                    xc_out = xc[:].rearrange("p (dc w) -> p dc w", dc=DC)[:, dc0:dc0 + 8, 3:3 + T]
                    p.step("scalar",
                           (lambda xc_out=xc_out: lambda s: s.copy(
                               xc_out, ps_xz[:].rearrange("p (e t) -> p e t", e=8)))(),
                           waits=[(s_t, mm_done), (s_v, halo_done)], inc=(s_s, 1))
                    xz_evac_mark = s_s.n
                    xc_done = s_s.n
                    for dc in range(dc0, dc0 + 8):
                        o = dc * (T + 3)
                        ca = convacc[dc % 2]
                        p.step("vector",
                               (lambda o=o, dc=dc, ca=ca: lambda v: v.tensor_scalar(
                                   ca[:, 0:T], xc[:, o:o + T], col_cw(dc, 0), None, OP.mult))(),
                               waits=[(s_s, max(xc_done, prev_u_done[dc % 2]))])
                        for k in (1, 2, 3):
                            src = ca[:, 0:T] if k % 2 == 1 else ca[:, T:2 * T]
                            dst = ca[:, T:2 * T] if k % 2 == 1 else ca[:, 0:T]
                            p.step("vector",
                                   (lambda o=o, dc=dc, k=k, src=src, dst=dst: lambda v: v.scalar_tensor_tensor(
                                       dst, xc[:, o + k:o + k + T], col_cw(dc, k), src,
                                       OP.mult, OP.add))(),
                                   inc=((s_v, 1) if k == 3 else None))
                        conv_done = s_v.n
                        p.step("scalar",
                               (lambda dc=dc, par=par, ca=ca: lambda s: s.activation(
                                   out=u_sb[par][:, dc * T:(dc + 1) * T], in_=ca[:, T:2 * T],
                                   func=AF.Silu, bias=col_cb(dc)))(),
                               waits=[(s_v, conv_done)], inc=(s_s, 1))
                        u_done = s_s.n
                        prev_u_done[dc % 2] = u_done
                        p.step("tensor",
                               (lambda dc=dc, par=par: lambda t: t.matmul(
                                   ps_misc[0:33, 0:T], wx_sb[:, dc * 33:(dc + 1) * 33],
                                   u_sb[par][:, dc * T:(dc + 1) * T],
                                   start=(dc == 0), stop=(dc == DC - 1)))(),
                               waits=[(s_s, u_done if dc > 0 else max(u_done, misc_free_at_entry))],
                               inc=((s_t, 1) if dc == DC - 1 else None))
                else:
                    dcz0 = (eg - 2) * 8
                    p.step("scalar",
                           (lambda dcz0=dcz0, par=par: lambda s: s.activation(
                               out=sz_sb[par][:, dcz0 * T:(dcz0 + 8) * T], in_=ps_xz[:], func=AF.Silu))(),
                           waits=[(s_t, mm_done), (s_v, sz_free_v[par])], inc=(s_s, 1))
                    xz_evac_mark = s_s.n

        def phase_A_tail(tau, st):
            nonlocal misc_free_s
            par = st["par"]
            xw = st["xw"]
            xdbl_mm_done = s_t.n
            p.step("scalar", lambda s: s.copy(xdbl_sb[:], ps_misc[0:33, 0:T]),
                   waits=[(s_t, xdbl_mm_done)], inc=(s_s, 1))
            xdbl_done = s_s.n
            misc_free_s = xdbl_done
            p.step("gpsimd", (lambda tau=tau: lambda e: e.dma_start(
                out=d_scr33[tau % 2], in_=xdbl_sb[:]))(),
                   waits=[(s_s, xdbl_done)], inc=(s_scr33, 16))
            scr_w = s_scr33.n
            p.step("gpsimd", (lambda par=par, tau=tau: lambda e: e.dma_start(
                out=dtraw_rep[par][:], in_=d_scr33[tau % 2, 0:1, :].broadcast_to([128, T])))(),
                   waits=[(s_scr33, scr_w), (s_s, dtr_free_s[par])], inc=(s_dtr, 16))
            p.step("gpsimd", (lambda par=par, tau=tau: lambda e: e.dma_start(
                out=brep[par][:],
                in_=d_scr33[tau % 2, 1:17, :].rearrange("a b -> (a b)")[None, :].broadcast_to([128, NSTATE * T])))(),
                   waits=[(s_v, brep_free_p[par])], inc=(s_bc, 16))
            p.step("gpsimd", (lambda par=par, tau=tau: lambda e: e.dma_start(
                out=crep[par][:],
                in_=d_scr33[tau % 2, 17:33, :].rearrange("a b -> (a b)")[None, :].broadcast_to([128, NSTATE * T])))(),
                   waits=[(s_v, crep_free_p[par])], inc=(s_cc, 16))
            return dict(dtr=s_dtr.n, bc=s_bc.n, cc=s_cc.n, xw=xw, pool_evac=pool_evac_v)

        prol = {}

        def phase_B_prol(t2, mk):
            par = t2 % 2
            dtrd = mk["dtr"]
            for i in range(DG):
                p.step("scalar",
                       (lambda i=i, par=par: lambda s: s.activation(
                           out=dts[:, i * T:(i + 1) * T], in_=dtraw_rep[par][:], func=AF.Exp,
                           scale=col_wdt(i), bias=col_bdt(i)))(),
                       waits=([(s_dtr, dtrd)] if i == 0 else []))
            p.step("scalar", lambda s: s.activation(
                   out=dt_g[0][:], in_=dts[:], func=AF.Ln, bias=ones_c),
                   waits=[(s_v, dtu_vmark[0])], inc=(s_s, 1))
            lnd = s_s.n
            prol[t2] = lnd

        def phase_B(tau, marks, inter=None):
            nonlocal prev_y3_v, stat_free_s
            par = tau % 2
            dtrd, bcd, ccd = marks["dtr"], marks["bc"], marks["cc"]
            pend_stats = [None]
            sq_t_mark = 0

            lnd_g = {}

            def emit_dt_s(gg):
                dgi = gg % 2
                dc0 = gg * DG
                for i in range(DG):
                    dc = dc0 + i
                    p.step("scalar",
                           (lambda dc=dc, i=i, par=par: lambda s: s.activation(
                               out=dts[:, i * T:(i + 1) * T], in_=dtraw_rep[par][:], func=AF.Exp,
                               scale=col_wdt(dc), bias=col_bdt(dc)))(),
                           waits=([(s_dtr, dtrd)] if i == 0 else []))
                p.step("scalar", (lambda dgi=dgi: lambda s: s.activation(
                       out=dt_g[dgi][:], in_=dts[:], func=AF.Ln, bias=ones_c))(),
                       waits=[(s_v, dtu_vmark[dgi])], inc=(s_s, 1))
                lnd_g[gg] = s_s.n

            def emit_dt_v(gg):
                dgi = gg % 2
                dc0 = gg * DG
                p.step("vector",
                       (lambda dgi=dgi, dc0=dc0, par=par: lambda v: v.tensor_tensor(
                           dtu_pp[dgi][:], dt_g[dgi][:], u_sb[par][:, dc0 * T:(dc0 + DG) * T], OP.mult))(),
                       waits=[(s_s, lnd_g[gg])], inc=(s_v, 1))
                dtu_vmark[dgi] = s_v.n

            dA_done = {}
            dBu_pd = {}

            def emit_dA(gg, gh):
                buf = (gg * NGH + gh) % 2
                dgi = gg % 2
                for j in range(HN):
                    n = gh * HN + j
                    dA_out = dA_pp[buf][:].rearrange("p (dc n t) -> p n dc t", dc=DG, n=HN)[:, j]
                    p.step("scalar",
                           (lambda dA_out=dA_out, n=n, dgi=dgi: lambda s: s.activation(
                               out=dA_out, in_=dt_g[dgi][:], func=AF.Exp, scale=a_tmp[:, n:n + 1]))(),
                           waits=[(s_v, dA_free[buf])],
                           inc=((s_s, 1) if j == HN - 1 else None))
                dA_done[(gg, gh)] = s_s.n

            def emit_dbu(gg, gh):
                pb = gh % 2
                dgi = gg % 2
                bsl = brep[par][:].rearrange("p (n t) -> p n t", n=NSTATE)[:, gh * HN:(gh + 1) * HN]
                dBu_out = dBu_pp[pb][:].rearrange("p (dc n t) -> p dc n t", dc=DG, n=HN)
                dtu_b = dtu_pp[dgi][:].rearrange("p (dc t) -> p dc t", dc=DG)[:, :, None, :].broadcast_to([128, DG, HN, T])
                bsl_b = bsl[:, None, :, :].broadcast_to([128, DG, HN, T])
                p.step("vector",
                       (lambda dBu_out=dBu_out, dtu_b=dtu_b, bsl_b=bsl_b: lambda v: v.tensor_tensor(
                           dBu_out, dtu_b, bsl_b, OP.mult))(),
                       waits=[(s_bc, bcd)], inc=(s_v, 1))
                dBu_pd[(gg, gh)] = s_v.n

            if tau in prol:
                lnd_g[0] = prol.pop(tau)
            else:
                emit_dt_s(0)
            emit_dA(0, 0)
            emit_dt_v(0)
            emit_dbu(0, 0)
            for g in range(NDG):
                dgi = g % 2
                dc0 = g * DG
                if g + 1 < NDG:
                    emit_dt_s(g + 1)
                pend = [None]

                def emit_tree(ent):
                    buf, pb, gh, cm_done, cm_eng = ent
                    gA = dA_pp[buf][:].rearrange("p (dc n t) -> p dc n t", dc=DG, n=HN)
                    l1 = dBu_pp[pb][:, 0:SW // 2].rearrange("p (dc n t) -> p dc n t", dc=DG, n=HN // 2)
                    slot = PTA if gh % 2 == 0 else PTB
                    tw = [(s_p, cm_done)] if cm_eng == "p" else []
                    p.step("vector",
                           (lambda gA=gA, l1=l1: lambda v: v.tensor_tensor(
                               l1, gA[:, :, 0:HN // 2], gA[:, :, HN // 2:HN], OP.add))(),
                           waits=tw, inc=(s_v, 1))
                    dA_free[buf] = s_v.n
                    p.step("vector",
                           (lambda l1=l1, slot=slot: lambda v: v.tensor_tensor(
                               part[:, slot:slot + DG * T].rearrange("p (dc t) -> p dc t", dc=DG),
                               l1[:, :, 0:1, :].rearrange("p dc n t -> p dc (n t)"),
                               l1[:, :, 1:2, :].rearrange("p dc n t -> p dc (n t)"),
                               OP.add))(),
                           inc=(s_v, 1))
                    dBu_free[pb] = s_v.n
                    if gh == 1:
                        p.step("vector", lambda v: v.tensor_tensor(
                            part[:, PTC:PTC + DG * T], part[:, PTA:PTA + DG * T], part[:, PTB:PTB + DG * T], OP.add))
                    elif gh == 3:
                        p.step("vector", lambda v: v.tensor_tensor(
                            part[:, PTA:PTA + DG * T], part[:, PTA:PTA + DG * T], part[:, PTB:PTB + DG * T], OP.add))

                for gh in range(NGH):
                    buf = (g * NGH + gh) % 2
                    pb = gh % 2
                    sts4 = state[:].rearrange("p (dc n) -> p dc n", dc=DC)[:, dc0:dc0 + DG, gh * HN:(gh + 1) * HN][:, :, :, None]
                    dA_fc = dA_pp[buf][:].rearrange("p (dc n t) -> p dc n t", dc=DG, n=HN)[:, :, :, 0:1]
                    dBu_fc = dBu_pp[pb][:].rearrange("p (dc n t) -> p dc n t", dc=DG, n=HN)[:, :, :, 0:1]
                    c32_4 = c32[:].rearrange("p (dc n) -> p dc n", dc=DG)[:, :, :, None]
                    p.step("vector",
                           (lambda dA_fc=dA_fc, sts4=sts4, c32_4=c32_4: lambda v: v.tensor_tensor(c32_4, dA_fc, sts4, OP.mult))(),
                           waits=[(s_s, dA_done[(g, gh)])])
                    p.step("vector",
                           (lambda buf=buf: lambda v: v.memset(
                               dA_pp[buf][:].rearrange("p (s t) -> p s t", s=DG * HN)[:, :, 0:1], 0.0))())
                    p.step("vector",
                           (lambda pb=pb: lambda v: v.tensor_scalar(
                               dBu_pp[pb][:], dBu_pp[pb][:], -10.0, 10.0, OP.max, OP.min))())
                    p.step("vector",
                           (lambda dBu_fc=dBu_fc, c32_4=c32_4: lambda v: v.tensor_tensor(dBu_fc, dBu_fc, c32_4, OP.add))())
                    vspace()
                    hb = (g * NGH + gh) % 2
                    p.step("vector",
                           (lambda buf=buf, pb=pb, hb=hb: lambda v: v.tensor_tensor_scan(
                               h_pp[hb][:], dA_pp[buf][:], dBu_pp[pb][:], 0.0, OP.mult, OP.add))(),
                           inc=(s_v, 1))
                    scan_done = s_v.n
                    if gh == 0 and g + 1 < NDG:
                        emit_dt_v(g + 1)
                    had_tree = pend[0] is not None
                    if had_tree:
                        emit_tree(pend[0])
                        pend[0] = None
                    nxt = (g, gh + 1) if gh + 1 < NGH else ((g + 1, 0) if g + 1 < NDG else None)
                    if nxt is not None:
                        emit_dA(*nxt)
                        emit_dbu(*nxt)
                    csl = crep[par][:].rearrange("p (n t) -> p n t", n=NSTATE)[:, gh * HN:(gh + 1) * HN]
                    csl_b = csl[:, None, :, :].broadcast_to([128, DG, HN, T])
                    h4 = h_pp[hb][:].rearrange("p (dc n t) -> p dc n t", dc=DG, n=HN)
                    gP = dA_pp[buf][:].rearrange("p (dc n t) -> p dc n t", dc=DG, n=HN)
                    p.step("vector",
                           (lambda gP=gP, h4=h4, csl_b=csl_b: lambda v: v.tensor_tensor(
                               gP, h4, csl_b, OP.mult))(),
                           waits=[(s_cc, ccd)], inc=(s_v, 1))
                    cm_done, cm_eng = s_v.n, "v"
                    h_lc = h_pp[hb][:].rearrange("p (dc n t) -> p dc n t", dc=DG, n=HN)[:, :, :, T - 1:T]
                    p.step("vector",
                           (lambda h_lc=h_lc, sts4=sts4: lambda v: v.tensor_copy(sts4, h_lc))())
                    pend[0] = (buf, pb, gh, cm_done, cm_eng)
                    if gh == 1:
                        if inter is not None:
                            inter(2 * g)
                        if pend_stats[0] is not None:
                            pend_stats[0]()
                            pend_stats[0] = None
                emit_tree(pend[0])
                pend[0] = None
                p.step("vector",
                       (lambda dc0=dc0: lambda v: v.tensor_tensor(
                           y_sb[:, dc0 * T:(dc0 + DG) * T], part[:, PTA:PTA + DG * T], part[:, PTC:PTC + DG * T], OP.add))(),
                       waits=[(s_t, y_free_t)], inc=(s_v, 1))
                y_done = s_v.n
                p.step("scalar",
                       (lambda dc0=dc0: lambda s: s.activation(
                           out=sqy[:], in_=y_sb[:, dc0 * T:(dc0 + DG) * T], func=AF.Square))(),
                       waits=[(s_v, y_done), (s_t, sq_t_mark)], inc=(s_s, 1))
                sqd = s_s.n
                if inter is not None:
                    inter(2 * g + 1)

                def emit_stats(g=g, dc0=dc0, y_done=y_done, sqd=sqd):
                    nonlocal sq_t_mark
                    for i in range(DG):
                        dc = dc0 + i
                        p.step("tensor",
                               (lambda dc=dc: lambda t: t.matmul(
                                   ps_stat[0:1, 0:T], ones_c, y_sb[:, dc * T:(dc + 1) * T],
                                   start=(dc == 0), stop=(dc == DC - 1)))(),
                               waits=[(s_v, max(y_done, pool_evac_v)), (s_s, stat_free_s)])
                        p.step("tensor",
                               (lambda i=i, dc=dc: lambda t: t.matmul(
                                   ps_stat[32:33, 0:T], ones_c, sqy[:, i * T:(i + 1) * T],
                                   start=(dc == 0), stop=(dc == DC - 1)))(),
                               waits=[(s_s, sqd)],
                               inc=((s_t, 1) if i == DG - 1 else None))
                    sq_t_mark = s_t.n

                pend_stats[0] = emit_stats
            if pend_stats[0] is not None:
                pend_stats[0]()
                pend_stats[0] = None
            stats_done_t = s_t.n
            p.step("scalar", lambda s: s.activation(out=stat_sb[0:1, :], in_=ps_stat[0:1, 0:T],
                                                    func=AF.Copy, scale=rden_col[0:1, :]),
                   waits=[(s_t, stats_done_t)], inc=(s_s, 1))
            p.step("scalar", lambda s: s.activation(out=stat_sb[32:33, :], in_=ps_stat[32:33, 0:T],
                                                    func=AF.Copy, scale=rden_col[32:33, :]),
                   inc=(s_s, 1))
            fin_s = s_s.n
            stat_free_s = fin_s
            if inter is not None:
                inter(8)
            p.step("gpsimd", dma(d_scr2[0:1, :], stat_sb[0:1, :]), waits=[(s_s, fin_s)], inc=(s_sc2, 16))
            p.step("gpsimd", dma(d_scr2[1:2, :], stat_sb[32:33, :]), inc=(s_sc2, 16))
            sc2w = s_sc2.n
            p.step("gpsimd", lambda e: e.dma_start(
                out=m2r_rep[:], in_=d_scr2[:].rearrange("a b -> (a b)")[None, :].broadcast_to([128, 2 * T])),
                   waits=[(s_sc2, sc2w)], inc=(s_m2r, 16))
            m2rd = s_m2r.n
            p.step("vector", lambda v: v.tensor_tensor(sqy[:, 0:T], m2r_rep[:, 0:T], m2r_rep[:, 0:T], OP.mult),
                   waits=[(s_m2r, m2rd)])
            p.step("vector", lambda v: v.tensor_tensor(sqy[:, T:2 * T], m2r_rep[:, T:2 * T], sqy[:, 0:T], OP.subtract),
                   inc=(s_v, 1))
            var2d = s_v.n
            p.step("scalar", lambda s: s.activation(out=sqy[:, 2 * T:3 * T], in_=sqy[:, T:2 * T],
                                                    func=AF.Sqrt, bias=eps_col),
                   waits=[(s_v, var2d)], inc=(s_s, 1))
            sd2d = s_s.n
            p.step("vector", lambda v: v.reciprocal(rstd2_rep[:], sqy[:, 2 * T:3 * T]),
                   waits=[(s_s, sd2d)])
            vspace()
            mu_b = m2r_rep[:, 0:T][:, None, :].broadcast_to([128, DG, T])
            rs_b = rstd2_rep[:][:, None, :].broadcast_to([128, DG, T])
            for g in range(NDG):
                dc0 = g * DG
                ysl = y_sb[:, dc0 * T:(dc0 + DG) * T]
                p.step("vector",
                       (lambda ysl=ysl, mu_b=mu_b: lambda v: v.tensor_tensor(
                           sqy[:].rearrange("p (dc t) -> p dc t", dc=DG),
                           ysl.rearrange("p (dc t) -> p dc t", dc=DG), mu_b, OP.subtract))())
                p.step("vector",
                       (lambda rs_b=rs_b: lambda v: v.tensor_tensor(
                           dtmp[:].rearrange("p (dc t) -> p dc t", dc=DG),
                           sqy[:].rearrange("p (dc t) -> p dc t", dc=DG), rs_b, OP.mult))())
                for i in range(DG):
                    dc = dc0 + i
                    p.step("vector",
                           (lambda i=i, dc=dc: lambda v: v.tensor_scalar(
                               sqy[:, i * T:(i + 1) * T], dtmp[:, i * T:(i + 1) * T],
                               col_g(dc), col_b(dc), OP.mult, OP.add))())
                    p.step("vector",
                           (lambda i=i, dc=dc, par=par: lambda v: v.scalar_tensor_tensor(
                               dtmp[:, i * T:(i + 1) * T], u_sb[par][:, dc * T:(dc + 1) * T], col_D(dc),
                               sqy[:, i * T:(i + 1) * T], OP.mult, OP.add))())
                p.step("vector",
                       (lambda ysl=ysl, dc0=dc0, par=par: lambda v: v.tensor_tensor(
                           ysl, dtmp[:], sz_sb[par][:, dc0 * T:(dc0 + DG) * T], OP.mult))(),
                       inc=(s_v, 1))
            y3_done = s_v.n
            u_free_v[par] = y3_done
            sz_free_v[par] = y3_done
            brep_free_p[par] = y3_done
            crep_free_p[par] = y3_done
            dtr_free_s[par] = s_s.n
            prev_y3_v = y3_done
            return y3_done

        cout = {}
        cev = {}
        outps_free_v = [0]

        def phase_C_mm(tau, y3_done):
            nonlocal y_free_t
            par = tau % 2
            for dc in range(DC):
                q = dc % NWB
                p.step("gpsimd", (lambda q=q, dc=dc: lambda e: e.dma_start(
                    out=wbuf[q], in_=d_woutT[dc * 128:(dc + 1) * 128, :]))(),
                       waits=[(s_t, wfree[q])], inc=(s_wc[q], 16))
                wcmark = s_wc[q].n
                for m in range(8):
                    p.step("tensor",
                           (lambda q=q, dc=dc, m=m: lambda t: t.matmul(
                               ps_out[:, m * 128:(m + 1) * 128], wbuf[q][:, m * 128:(m + 1) * 128],
                               y_sb[:, dc * T:(dc + 1) * T],
                               start=(dc == 0 and m % 4 == 0), stop=False))(),
                           waits=[(s_wc[q], wcmark), (s_v, max(y3_done, outps_free_v[0]))],
                           inc=((s_t, 1) if m == 7 else None))
                wfree[q] = s_t.n
            for dm in range(DM):
                q = dm % NWB
                p.step("gpsimd", (lambda q=q, dm=dm: lambda e: e.dma_start(
                    out=wbuf[q], in_=d_ffnT[dm * 128:(dm + 1) * 128, :]))(),
                       waits=[(s_t, wfree[q])], inc=(s_wc[q], 16))
                wcmark = s_wc[q].n
                for m in range(8):
                    p.step("tensor",
                           (lambda q=q, dm=dm, m=m, par=par: lambda t: t.matmul(
                               ps_out[:, m * 128:(m + 1) * 128], wbuf[q][:, m * 128:(m + 1) * 128],
                               xn_fm[par][:, dm * T:(dm + 1) * T],
                               start=False, stop=False))(),
                           waits=[(s_wc[q], wcmark)],
                           inc=((s_t, 1) if m == 7 else None))
                wfree[q] = s_t.n
            xn_free_t[par] = s_t.n
            for m in range(8):
                p.step("tensor",
                       (lambda m=m, par=par: lambda t: t.matmul(
                           ps_out[:, m * 128:(m + 1) * 128], disha_sb[:, m * 128:(m + 1) * 128],
                           pooled_sb[par][:],
                           start=False, stop=(m % 4 == 3)))(),
                       inc=((s_t, 1) if m == 7 else None))
            out_mm = s_t.n
            y_free_t = out_mm
            pool_free_t[par] = out_mm
            cout[tau] = out_mm

        def phase_C_evac(tau):
            out_mm = cout.pop(tau)
            p.step("vector", lambda v: v.tensor_copy(fmout[:], ps_out[:]),
                   waits=[(s_t, out_mm)], inc=(s_v, 1))
            cev[tau] = s_v.n
            outps_free_v[0] = cev[tau]

        def phase_C_rest(tau, marks):
            nonlocal store_done
            par = tau % 2
            t0 = tau * T
            ev = cev.pop(tau)
            for m in range(8):
                bi = m % 2
                te, tm = tr_last[bi]
                tw = [(s_v, ev)] + ([(s_s, tm)] if te == "s" else [(s_v, tm)])
                p.step("tensor",
                       (lambda m=m, bi=bi: lambda t: t.transpose(
                           ps_tr[bi][:, 0:128], fmout[:, m * 128:(m + 1) * 128], ident))(),
                       waits=tw, inc=(s_t, 1))
                trd = s_t.n
                p.step("vector",
                       (lambda m=m, bi=bi, par3=tau % 3: lambda v: v.tensor_tensor(
                           out_tok[:, m * 128:(m + 1) * 128], ps_tr[bi][0:128, 0:128],
                           xstage[par3][:, m * 128:(m + 1) * 128], OP.add))(),
                       waits=[(s_t, trd), (s_x, marks["xw"]), (s_st, store_done)],
                       inc=(s_v, 1))
                tr_last[bi] = ("v", s_v.n)
            adds_done = s_v.n
            x_free_v[tau % 3] = adds_done
            p.step("sync", (lambda t0=t0: lambda e: e.dma_start(
                out=d_out[t0:t0 + T, :], in_=out_tok[:]))(),
                   waits=[(s_v, adds_done)], inc=(s_st, 16))
            store_done = s_st.n


        marks = {}
        y3s = {}
        st0 = phase_A_head(0)
        phase_A_pool(st0)
        for e in range(4):
            phase_A_eg_mm(0, e, st0)
            phase_A_eg_ev(0, e, st0)
        marks[0] = phase_A_tail(0, st0)
        for tau in range(NT):
            pieces = []
            if tau >= 1:
                phase_C_mm(tau - 1, y3s[tau - 1])
            if tau + 1 < NT:
                cell = {}
                pieces.append((lambda cell=cell, t2=tau + 1: (lambda: cell.__setitem__("st", phase_A_head(t2))))())
                pieces.append((lambda cell=cell, t2=tau + 1: (lambda: phase_A_eg_mm(t2, 0, cell["st"])))())
            if tau >= 1:
                pieces.append((lambda t1=tau - 1: (lambda: phase_C_evac(t1)))())
            if tau + 1 < NT:
                pieces.append((lambda cell=cell: (lambda: phase_A_pool(cell["st"])))())
                pieces.append((lambda cell=cell, t2=tau + 1: (lambda: phase_A_eg_ev(t2, 0, cell["st"])))())
                pieces.append((lambda cell=cell, t2=tau + 1: (lambda: phase_A_eg_mm(t2, 1, cell["st"])))())
            if tau >= 1:
                pieces.append((lambda t1=tau - 1: (lambda: phase_C_rest(t1, marks[t1])))())
            if tau + 1 < NT:
                for e in range(1, 4):
                    pieces.append((lambda e=e, cell=cell, t2=tau + 1: (lambda: phase_A_eg_ev(t2, e, cell["st"])))())
                    if e + 1 < 4:
                        pieces.append((lambda e=e, cell=cell, t2=tau + 1: (lambda: phase_A_eg_mm(t2, e + 1, cell["st"])))())
                pieces.append((lambda cell=cell, t2=tau + 1: (lambda: marks.__setitem__(t2, phase_A_tail(t2, cell["st"])))) ())
                pieces.append((lambda t2=tau + 1: (lambda: phase_B_prol(t2, marks[t2])))())

            TAKES = [2, 2, 2, 3, 2, 1, 1, 0, 99]

            def inter(k, pieces=pieces, TAKES=TAKES):
                for _ in range(TAKES[k]):
                    if pieces:
                        pieces.pop(0)()

            y3s[tau] = phase_B(tau, marks[tau], inter)
        phase_C_mm(NT - 1, y3s[NT - 1])
        phase_C_evac(NT - 1)
        phase_C_rest(NT - 1, marks[NT - 1])

        final_store = s_st.n
        p.step("sync", lambda e: e.nop(), waits=[(s_st, final_store)])

        with nc.Block() as block:
            @block.sync
            def _(sync):
                p.emit("sync", sync)

            @block.vector
            def _(vector):
                p.emit("vector", vector)

            @block.scalar
            def _(scalar):
                p.emit("scalar", scalar)

            @block.tensor
            def _(tensor):
                p.emit("tensor", tensor)

            @block.gpsimd
            def _(gpsimd):
                p.emit("gpsimd", gpsimd)

    return nc


def _host_layout(inputs):
    W_in = np.ascontiguousarray(inputs["L0_attn_W_in"])
    W_out = np.ascontiguousarray(inputs["L0_attn_W_out"])
    ffn_w = np.ascontiguousarray(inputs["L0_ffn_weight"])
    winT = np.ascontiguousarray(W_in.T.astype(np.float32))
    woutT = np.ascontiguousarray(W_out.T.astype(np.float32))
    ffnT = np.ascontiguousarray(ffn_w.T.astype(np.float32))
    disha = np.ascontiguousarray(inputs["L0_ffn_disha"].astype(np.float32))

    W_x = inputs["L0_attn_W_x"]
    wx = np.ascontiguousarray(
        W_x.T.reshape(DC, 128, 33).transpose(1, 0, 2).reshape(128, DC * 33).astype(np.float32))

    def chanv(a):
        return a.reshape(DC, 128).T

    chan = np.zeros((128, 176), np.float32)
    chan[:, 0:16] = chanv(inputs["L0_attn_W_dt"][:, 0])
    chan[:, 16:32] = chanv(inputs["L0_attn_b_dt"])
    chan[:, 32:48] = chanv(inputs["L0_attn_conv_b"])
    chan[:, 48:64] = chanv(inputs["L0_attn_D_param"])
    chan[:, 64:80] = chanv(inputs["L0_attn_ln_g"])
    chan[:, 80:96] = chanv(inputs["L0_attn_ln_b"])
    cw = inputs["L0_attn_conv_w"][:, 0, :]
    chan[:, 96:160] = cw.reshape(DC, 128, 4).transpose(1, 0, 2).reshape(128, 64)
    chan[:, 160:176] = np.broadcast_to(inputs["L0_attn_A_log"], (128, 16))

    cons = np.zeros((128, 195), np.float32)
    cons[:, 0:64] = np.tile(np.eye(64, dtype=np.float32), (2, 1))
    cons[:, 64] = 1.0
    cons[:, 65:193] = np.eye(128, dtype=np.float32)
    cons[:, 193] = LN_EPS
    cons[:, 194] = 1.0 / DIN

    return dict(winT=winT, woutT=woutT, ffnT=ffnT, disha=disha, wx=wx,
                chan=chan, cons=cons)


def kernel(**inputs):
    shared = _host_layout(inputs)
    x = np.ascontiguousarray(inputs["L0_attn_x"].astype(np.float32))
    nc = build_program()
    in_maps = []
    for b in range(8):
        m = dict(shared)
        m["x"] = np.ascontiguousarray(x[b])
        in_maps.append(m)
    res = run_bass_kernel_spmd(nc, in_maps, core_ids=list(range(8)))
    out = np.stack([r["out"] for r in res.results], axis=0)
    return out.astype(np.float32)


# version 23
# speedup vs baseline: 1.0284x; 1.0015x over previous
import numpy as np
from contextlib import ExitStack

import concourse.bass as bass
import concourse.mybir as mybir
from concourse.bass_utils import run_bass_kernel_spmd

F32 = mybir.dt.float32
AF = mybir.ActivationFunctionType
OP = mybir.AluOpType
AX = mybir.AxisListType.X

L = 2048
DMODEL = 1024
DIN = 2048
NSTATE = 16
T = 128
NT = L // T
DC = DIN // 128
DM = DMODEL // 128
DG = 4
NDG = DC // DG
HN = 4
NGH = NSTATE // HN
SW = DG * HN * T
LN_EPS = 1e-5


class Sem:
    def __init__(self, h):
        self.h = h
        self.n = 0


class Prog:
    def __init__(self):
        self.steps = []

    def step(self, engine, fn, waits=(), inc=None):
        for s, v in waits:
            assert v <= s.n, f"wait on future sem value {v} > {s.n}"
        self.steps.append((engine, fn, list(waits), inc))
        if inc is not None:
            inc[0].n += inc[1]

    def emit(self, engine_name, eng):
        last_wait = {}
        for e, fn, waits, inc in self.steps:
            if e != engine_name:
                continue
            for s, v in waits:
                if last_wait.get(id(s), -1) >= v:
                    continue
                eng.wait_ge(s.h, v)
                last_wait[id(s)] = v
            inst = fn(eng)
            if inc is not None:
                assert inst is not None
                inst.then_inc(inc[0].h, inc[1])


def build_program():
    nc = bass.Bass("TRN2", target_bir_lowering=False, debug=False,
                   detect_race_conditions=False)

    d_x = nc.dram_tensor("x", [L, DMODEL], F32, kind="ExternalInput").ap()
    d_winT = nc.dram_tensor("winT", [DMODEL, 2 * DIN], F32, kind="ExternalInput").ap()
    d_woutT = nc.dram_tensor("woutT", [DIN, DMODEL], F32, kind="ExternalInput").ap()
    d_ffnT = nc.dram_tensor("ffnT", [DMODEL, DMODEL], F32, kind="ExternalInput").ap()
    d_disha = nc.dram_tensor("disha", [64, DMODEL], F32, kind="ExternalInput").ap()
    d_wx = nc.dram_tensor("wx", [128, DC * 33], F32, kind="ExternalInput").ap()
    d_chan = nc.dram_tensor("chan", [128, 176], F32, kind="ExternalInput").ap()
    d_cons = nc.dram_tensor("cons", [128, 195], F32, kind="ExternalInput").ap()
    d_out = nc.dram_tensor("out", [L, DMODEL], F32, kind="ExternalOutput").ap()
    d_scr33 = nc.dram_tensor("scr33", [2, 33, T], F32, kind="Internal").ap()
    d_scr2 = nc.dram_tensor("scr2", [2, T], F32, kind="Internal").ap()

    p = Prog()

    with ExitStack() as ctx:
        sb = lambda name, shape: ctx.enter_context(nc.sbuf_tensor(name, shape, F32)).ap()
        ps = lambda name, shape: ctx.enter_context(nc.psum_tensor(name, shape, F32)).ap()

        wbuf = [sb(f"wbuf{i}", [128, 1024]) for i in range(3)]
        NWB = len(wbuf)
        wx_sb = sb("wx_sb", [128, DC * 33])
        disha_sb = sb("disha_sb", [64, DMODEL])
        chan_sb = sb("chan_sb", [128, 176])
        cons_sb = sb("cons_sb", [128, 195])
        a_tmp = sb("a_tmp", [128, NSTATE])
        a_neg = sb("a_neg", [128, NSTATE])

        xstage = [sb(f"xstage{i}", [128, 1024]) for i in range(3)]
        xn_tok = sb("xn_tok", [128, 1024])
        xn_fm = [sb(f"xn_fm{i}", [128, DM * T]) for i in range(2)]
        xc = sb("xc", [128, DC * (T + 3)])
        u_sb = [sb(f"u{i}", [128, DC * T]) for i in range(2)]
        sz_sb = [sb(f"sz{i}", [128, DC * T]) for i in range(2)]
        y_sb = sb("y", [128, DC * T])
        brep = [sb(f"brep{i}", [128, NSTATE * T]) for i in range(2)]
        crep = [sb(f"crep{i}", [128, NSTATE * T]) for i in range(2)]
        dtraw_rep = [sb(f"dtraw{i}", [128, T]) for i in range(2)]
        m2r_rep = sb("m2r", [128, 2 * T])
        rstd2_rep = sb("rstd2", [128, T])
        dt_g = [sb(f"dtg{i}", [128, DG * T]) for i in range(2)]
        dA_pp = [sb(f"dA{i}", [128, SW]) for i in range(2)]
        dBu_pp = [sb(f"dBu{i}", [128, SW]) for i in range(2)]
        h_pp = [sb(f"h{i}", [128, SW]) for i in range(2)]
        part = sb("part", [128, 3 * DG * T])
        c32 = sb("c32", [128, DG * HN])
        state = sb("state", [128, DC * NSTATE])
        xdbl_sb = sb("xdbl_sb", [33, T])
        pooled_sb = [sb(f"pooled{i}", [64, T]) for i in range(2)]
        sqy = sb("sqy", [128, DG * T])
        fmout = sb("fmout", [128, 1024])
        out_tok = fmout
        PTA, PTB, PTC = 0, DG * T, 2 * DG * T
        convacc = [sb(f"convacc{i}", [128, 2 * T]) for i in range(2)]
        dtmp = sb("dtmp", [128, DG * T])
        dts = sb("dts", [128, DG * T])
        dtu_pp = [sb(f"dtu{i}", [128, DG * T]) for i in range(2)]
        sums = sb("sums", [128, 8])
        mu_tok = sb("mu_tok", [128, 2])
        rstd_tok = sb("rstd_tok", [128, 2])
        spc = sb("spc", [128, 128])
        stat_sb = sb("stat_sb", [33, T])

        ps_xz = ps("ps_xz", [128, 1024])
        ps_out = ps("ps_out", [128, 1024])
        ps_misc = ps("ps_misc", [128, 512])
        ps_stat = ps("ps_stat", [128, 512])
        ps_tr = [ps("ps_tr0", [128, 512]), ps("ps_tr1", [128, 512])]

        s_ld = Sem(ctx.enter_context(nc.semaphore("s_ld")))
        s_st = Sem(ctx.enter_context(nc.semaphore("s_st")))
        s_v = Sem(ctx.enter_context(nc.semaphore("s_v")))
        s_s = Sem(ctx.enter_context(nc.semaphore("s_s")))
        s_t = Sem(ctx.enter_context(nc.semaphore("s_t")))
        s_p = Sem(ctx.enter_context(nc.semaphore("s_p")))
        s_wb = [Sem(ctx.enter_context(nc.semaphore(f"s_wb{i}"))) for i in range(NWB)]
        s_wc = [Sem(ctx.enter_context(nc.semaphore(f"s_wc{i}"))) for i in range(NWB)]
        s_x = Sem(ctx.enter_context(nc.semaphore("s_x")))
        s_scr33 = Sem(ctx.enter_context(nc.semaphore("s_scr33")))
        s_dtr = Sem(ctx.enter_context(nc.semaphore("s_dtr")))
        s_bc = Sem(ctx.enter_context(nc.semaphore("s_bc")))
        s_cc = Sem(ctx.enter_context(nc.semaphore("s_cc")))
        s_sc2 = Sem(ctx.enter_context(nc.semaphore("s_sc2")))
        s_m2r = Sem(ctx.enter_context(nc.semaphore("s_m2r")))

        col_wdt = lambda dc: chan_sb[:, dc:dc + 1]
        col_bdt = lambda dc: chan_sb[:, 16 + dc:16 + dc + 1]
        col_cb = lambda dc: chan_sb[:, 32 + dc:32 + dc + 1]
        col_D = lambda dc: chan_sb[:, 48 + dc:48 + dc + 1]
        col_g = lambda dc: chan_sb[:, 64 + dc:64 + dc + 1]
        col_b = lambda dc: chan_sb[:, 80 + dc:80 + dc + 1]
        col_cw = lambda dc, k: chan_sb[:, 96 + dc * 4 + k:96 + dc * 4 + k + 1]
        alog = chan_sb[:, 160:176]
        mpool = cons_sb[:, 0:64]
        ones_c = cons_sb[:, 64:65]
        ident = cons_sb[:, 65:193]
        eps_col = cons_sb[:, 193:194]
        rden_col = cons_sb[:, 194:195]

        def dma(dst, src):
            return lambda eng: eng.dma_start(out=dst, in_=src)

        def vspace():
            p.step("vector", lambda v: v.tensor_copy(spc[:], cons_sb[:, 65:193]))

        wmark = [0] * NWB
        wfree = [0] * NWB
        tr_last = [("v", 0), ("v", 0)]
        u_free_v = [0, 0]
        sz_free_v = [0, 0]
        xn_free_t = [0, 0]
        x_free_v = [0, 0, 0]
        pool_free_t = [0, 0]
        brep_free_p = [0, 0]
        dtr_free_s = [0, 0]
        dA_free = [0, 0]
        dBu_free = [0, 0]
        crep_free_p = [0, 0]
        y_free_t = 0
        store_done = 0
        xz_evac_mark = 0
        misc_free_s = 0
        stat_free_s = 0
        pool_evac_v = 0
        dtu_vmark = [0, 0]
        xn_tok_free_t = 0
        prev_y3_v = 0
        last_ctr_t = 0
        prev_u_done = [0, 0]

        p.step("sync", dma(wx_sb, d_wx), inc=(s_ld, 16))
        p.step("sync", dma(disha_sb, d_disha), inc=(s_ld, 16))
        p.step("sync", dma(chan_sb, d_chan), inc=(s_ld, 16))
        p.step("sync", dma(cons_sb, d_cons), inc=(s_ld, 16))
        ld_setup = s_ld.n
        p.step("vector", lambda v: v.memset(state[:], 0.0), waits=[(s_ld, ld_setup)])
        p.step("vector", lambda v: v.memset(xc[:], 0.0))
        p.step("vector", lambda v: v.tensor_scalar_min(a_tmp[:], alog, 5.0), inc=(s_v, 1))
        va = s_v.n
        p.step("scalar", lambda s: s.activation(out=a_neg[:], in_=a_tmp[:], func=AF.Exp),
               waits=[(s_v, va)], inc=(s_s, 1))
        sa = s_s.n
        p.step("vector", lambda v: v.tensor_scalar_mul(a_tmp[:], a_neg[:], -1.0),
               waits=[(s_s, sa)], inc=(s_v, 1))


        def phase_A_head(tau):
            nonlocal pool_evac_v, xn_tok_free_t
            par = tau % 2
            t0 = tau * T
            par3 = tau % 3
            st = dict(par=par, misc_free_at_entry=misc_free_s)
            p.step("sync", dma(xstage[par3][:], d_x[t0:t0 + T, :]),
                   waits=[(s_v, x_free_v[par3])], inc=(s_x, 16))
            xw = s_x.n
            for dc in range(DC):
                o = dc * (T + 3)
                p.step("vector",
                       (lambda o=o: lambda v: v.tensor_copy(xc[:, o:o + 3], xc[:, o + T:o + T + 3]))(),
                       inc=((s_v, 1) if dc == DC - 1 else None))
            halo_done = s_v.n
            p.step("vector", (lambda par3=par3: lambda v: v.reduce_sum(sums[:, 0:1], xstage[par3][:], AX))(),
                   waits=[(s_x, xw)])
            p.step("scalar",
                   (lambda par=par, par3=par3: lambda s: s.activation(out=u_sb[par][:, 0:1024], in_=xstage[par3][:], func=AF.Square))(),
                   waits=[(s_x, xw), (s_v, u_free_v[par])], inc=(s_s, 1))
            sqd = s_s.n
            p.step("vector",
                   (lambda par=par: lambda v: v.reduce_sum(sums[:, 1:2], u_sb[par][:, 0:1024], AX))(),
                   waits=[(s_s, sqd)])
            p.step("vector", lambda v: v.tensor_scalar_mul(mu_tok[:, 0:1], sums[:, 0:1], 1.0 / DMODEL))
            vspace()
            p.step("vector", lambda v: v.tensor_tensor(sums[:, 2:3], mu_tok[:, 0:1], mu_tok[:, 0:1], OP.mult))
            p.step("vector", lambda v: v.tensor_scalar_mul(sums[:, 3:4], sums[:, 1:2], 1.0 / DMODEL))
            vspace()
            p.step("vector", lambda v: v.tensor_tensor(sums[:, 3:4], sums[:, 3:4], sums[:, 2:3], OP.subtract),
                   inc=(s_v, 1))
            var_done = s_v.n
            p.step("scalar", lambda s: s.activation(out=sums[:, 2:3], in_=sums[:, 3:4], func=AF.Sqrt, bias=eps_col),
                   waits=[(s_v, var_done)], inc=(s_s, 1))
            sd_done = s_s.n
            p.step("vector", lambda v: v.reciprocal(rstd_tok[:, 0:1], sums[:, 2:3]),
                   waits=[(s_s, sd_done)])
            vspace()
            p.step("vector",
                   (lambda par3=par3: lambda v: v.tensor_scalar(
                       xn_tok[:], xstage[par3][:], mu_tok[:, 0:1], rstd_tok[:, 0:1],
                       OP.subtract, OP.mult))(),
                   waits=[(s_t, xn_tok_free_t)], inc=(s_v, 1))
            xn_done = s_v.n
            for dm in range(DM):
                bi = dm % 2
                te, tm = tr_last[bi]
                tw = [(s_v, xn_done)] + ([(s_s, tm)] if te == "s" else [(s_v, tm)])
                p.step("tensor",
                       (lambda dm=dm, bi=bi: lambda t: t.transpose(
                           ps_tr[bi][:, 0:128], xn_tok[:, dm * 128:(dm + 1) * 128], ident))(),
                       waits=tw, inc=(s_t, 1))
                trd = s_t.n
                p.step("scalar",
                       (lambda dm=dm, bi=bi, par=par: lambda s: s.copy(
                           xn_fm[par][:, dm * T:(dm + 1) * T], ps_tr[bi][:, 0:128]))(),
                       waits=[(s_t, trd), (s_t, xn_free_t[par])], inc=(s_s, 1))
                tr_last[bi] = ("s", s_s.n)
            xn_tok_free_t = s_t.n
            xn_fm_done = s_s.n
            st.update(xw=xw, halo_done=halo_done, xn_fm_done=xn_fm_done)
            return st

        def phase_A_pool(st):
            nonlocal pool_evac_v
            par = st["par"]
            xn_fm_done = st["xn_fm_done"]
            for dm in range(DM):
                p.step("tensor",
                       (lambda dm=dm, par=par: lambda t: t.matmul(
                           ps_stat[0:64, T:2 * T], mpool, xn_fm[par][:, dm * T:(dm + 1) * T],
                           start=(dm == 0), stop=(dm == DM - 1)))(),
                       waits=[(s_s, max(xn_fm_done, stat_free_s))],
                       inc=((s_t, 1) if dm == DM - 1 else None))
            pool_done_t = s_t.n
            p.step("vector",
                   (lambda par=par: lambda v: v.tensor_copy(pooled_sb[par][:], ps_stat[0:64, T:2 * T]))(),
                   waits=[(s_t, pool_done_t), (s_t, pool_free_t[par])], inc=(s_v, 1))
            pool_evac_v = s_v.n

        def phase_A_eg_mm(tau, eg, st):
            nonlocal xz_evac_mark
            par = st["par"]
            xn_fm_done = st["xn_fm_done"]
            for dm in range(DM):
                q = (eg * DM + dm) % NWB
                p.step("sync", (lambda q=q, dm=dm, eg=eg: lambda e: e.dma_start(
                    out=wbuf[q],
                    in_=d_winT[dm * 128:(dm + 1) * 128, eg * 1024:(eg + 1) * 1024]))(),
                       waits=[(s_t, wfree[q])], inc=(s_wb[q], 16))
                wmark[q] = s_wb[q].n
                for e in range(8):
                    p.step("tensor",
                           (lambda q=q, dm=dm, e=e, par=par: lambda t: t.matmul(
                               ps_xz[:, e * T:(e + 1) * T],
                               wbuf[q][:, e * 128:(e + 1) * 128],
                               xn_fm[par][:, dm * T:(dm + 1) * T],
                               start=(dm == 0 and e % 4 == 0),
                               stop=(dm == DM - 1 and e % 4 == 3)))(),
                           waits=[(s_wb[q], wmark[q]), (s_s, max(xn_fm_done, xz_evac_mark))],
                           inc=((s_t, 1) if e == 7 else None))
                wfree[q] = s_t.n
            st[f"mm{eg}"] = s_t.n

        def phase_A_eg_ev(tau, eg, st):
            nonlocal xz_evac_mark, prev_u_done
            par = st["par"]
            halo_done = st["halo_done"]
            misc_free_at_entry = st["misc_free_at_entry"]
            mm_done = st[f"mm{eg}"]
            if True:
                if eg < 2:
                    dc0 = eg * 8
                    xc_out = xc[:].rearrange("p (dc w) -> p dc w", dc=DC)[:, dc0:dc0 + 8, 3:3 + T]
                    p.step("scalar",
                           (lambda xc_out=xc_out: lambda s: s.copy(
                               xc_out, ps_xz[:].rearrange("p (e t) -> p e t", e=8)))(),
                           waits=[(s_t, mm_done), (s_v, halo_done)], inc=(s_s, 1))
                    xz_evac_mark = s_s.n
                    xc_done = s_s.n
                    for dc in range(dc0, dc0 + 8):
                        o = dc * (T + 3)
                        ca = convacc[dc % 2]
                        p.step("vector",
                               (lambda o=o, dc=dc, ca=ca: lambda v: v.tensor_scalar(
                                   ca[:, 0:T], xc[:, o:o + T], col_cw(dc, 0), None, OP.mult))(),
                               waits=[(s_s, max(xc_done, prev_u_done[dc % 2]))])
                        for k in (1, 2, 3):
                            src = ca[:, 0:T] if k % 2 == 1 else ca[:, T:2 * T]
                            dst = ca[:, T:2 * T] if k % 2 == 1 else ca[:, 0:T]
                            p.step("vector",
                                   (lambda o=o, dc=dc, k=k, src=src, dst=dst: lambda v: v.scalar_tensor_tensor(
                                       dst, xc[:, o + k:o + k + T], col_cw(dc, k), src,
                                       OP.mult, OP.add))(),
                                   inc=((s_v, 1) if k == 3 else None))
                        conv_done = s_v.n
                        p.step("scalar",
                               (lambda dc=dc, par=par, ca=ca: lambda s: s.activation(
                                   out=u_sb[par][:, dc * T:(dc + 1) * T], in_=ca[:, T:2 * T],
                                   func=AF.Silu, bias=col_cb(dc)))(),
                               waits=[(s_v, conv_done)], inc=(s_s, 1))
                        u_done = s_s.n
                        prev_u_done[dc % 2] = u_done
                        p.step("tensor",
                               (lambda dc=dc, par=par: lambda t: t.matmul(
                                   ps_misc[0:33, 0:T], wx_sb[:, dc * 33:(dc + 1) * 33],
                                   u_sb[par][:, dc * T:(dc + 1) * T],
                                   start=(dc == 0), stop=(dc == DC - 1)))(),
                               waits=[(s_s, u_done if dc > 0 else max(u_done, misc_free_at_entry))],
                               inc=((s_t, 1) if dc == DC - 1 else None))
                else:
                    dcz0 = (eg - 2) * 8
                    p.step("scalar",
                           (lambda dcz0=dcz0, par=par: lambda s: s.activation(
                               out=sz_sb[par][:, dcz0 * T:(dcz0 + 8) * T], in_=ps_xz[:], func=AF.Silu))(),
                           waits=[(s_t, mm_done), (s_v, sz_free_v[par])], inc=(s_s, 1))
                    xz_evac_mark = s_s.n

        def phase_A_tail(tau, st):
            nonlocal misc_free_s
            par = st["par"]
            xw = st["xw"]
            xdbl_mm_done = s_t.n
            p.step("scalar", lambda s: s.copy(xdbl_sb[:], ps_misc[0:33, 0:T]),
                   waits=[(s_t, xdbl_mm_done)], inc=(s_s, 1))
            xdbl_done = s_s.n
            misc_free_s = xdbl_done
            p.step("gpsimd", (lambda tau=tau: lambda e: e.dma_start(
                out=d_scr33[tau % 2], in_=xdbl_sb[:]))(),
                   waits=[(s_s, xdbl_done)], inc=(s_scr33, 16))
            scr_w = s_scr33.n
            p.step("gpsimd", (lambda par=par, tau=tau: lambda e: e.dma_start(
                out=dtraw_rep[par][:], in_=d_scr33[tau % 2, 0:1, :].broadcast_to([128, T])))(),
                   waits=[(s_scr33, scr_w), (s_s, dtr_free_s[par])], inc=(s_dtr, 16))
            p.step("gpsimd", (lambda par=par, tau=tau: lambda e: e.dma_start(
                out=brep[par][:],
                in_=d_scr33[tau % 2, 1:17, :].rearrange("a b -> (a b)")[None, :].broadcast_to([128, NSTATE * T])))(),
                   waits=[(s_v, brep_free_p[par])], inc=(s_bc, 16))
            p.step("gpsimd", (lambda par=par, tau=tau: lambda e: e.dma_start(
                out=crep[par][:],
                in_=d_scr33[tau % 2, 17:33, :].rearrange("a b -> (a b)")[None, :].broadcast_to([128, NSTATE * T])))(),
                   waits=[(s_v, crep_free_p[par])], inc=(s_cc, 16))
            return dict(dtr=s_dtr.n, bc=s_bc.n, cc=s_cc.n, xw=xw, pool_evac=pool_evac_v)

        prol = {}

        def phase_B_prol(t2, mk):
            par = t2 % 2
            dtrd = mk["dtr"]
            for i in range(DG):
                p.step("scalar",
                       (lambda i=i, par=par: lambda s: s.activation(
                           out=dts[:, i * T:(i + 1) * T], in_=dtraw_rep[par][:], func=AF.Exp,
                           scale=col_wdt(i), bias=col_bdt(i)))(),
                       waits=([(s_dtr, dtrd)] if i == 0 else []))
            p.step("scalar", lambda s: s.activation(
                   out=dt_g[0][:], in_=dts[:], func=AF.Ln, bias=ones_c),
                   waits=[(s_v, dtu_vmark[0])], inc=(s_s, 1))
            lnd = s_s.n
            prol[t2] = lnd

        def phase_B(tau, marks, inter=None):
            nonlocal prev_y3_v, stat_free_s
            par = tau % 2
            dtrd, bcd, ccd = marks["dtr"], marks["bc"], marks["cc"]
            pend_stats = [None]
            sq_t_mark = 0

            lnd_g = {}

            def emit_dt_s(gg):
                dgi = gg % 2
                dc0 = gg * DG
                for i in range(DG):
                    dc = dc0 + i
                    p.step("scalar",
                           (lambda dc=dc, i=i, par=par: lambda s: s.activation(
                               out=dts[:, i * T:(i + 1) * T], in_=dtraw_rep[par][:], func=AF.Exp,
                               scale=col_wdt(dc), bias=col_bdt(dc)))(),
                           waits=([(s_dtr, dtrd)] if i == 0 else []))
                p.step("scalar", (lambda dgi=dgi: lambda s: s.activation(
                       out=dt_g[dgi][:], in_=dts[:], func=AF.Ln, bias=ones_c))(),
                       waits=[(s_v, dtu_vmark[dgi])], inc=(s_s, 1))
                lnd_g[gg] = s_s.n

            def emit_dt_v(gg):
                dgi = gg % 2
                dc0 = gg * DG
                p.step("vector",
                       (lambda dgi=dgi, dc0=dc0, par=par: lambda v: v.tensor_tensor(
                           dtu_pp[dgi][:], dt_g[dgi][:], u_sb[par][:, dc0 * T:(dc0 + DG) * T], OP.mult))(),
                       waits=[(s_s, lnd_g[gg])], inc=(s_v, 1))
                dtu_vmark[dgi] = s_v.n

            dA_done = {}
            dBu_pd = {}

            def emit_dA(gg, gh):
                buf = (gg * NGH + gh) % 2
                dgi = gg % 2
                for j in range(HN):
                    n = gh * HN + j
                    dA_out = dA_pp[buf][:].rearrange("p (dc n t) -> p n dc t", dc=DG, n=HN)[:, j]
                    p.step("scalar",
                           (lambda dA_out=dA_out, n=n, dgi=dgi: lambda s: s.activation(
                               out=dA_out, in_=dt_g[dgi][:], func=AF.Exp, scale=a_tmp[:, n:n + 1]))(),
                           waits=[(s_v, dA_free[buf])],
                           inc=((s_s, 1) if j == HN - 1 else None))
                dA_done[(gg, gh)] = s_s.n

            def emit_dbu(gg, gh):
                pb = gh % 2
                dgi = gg % 2
                bsl = brep[par][:].rearrange("p (n t) -> p n t", n=NSTATE)[:, gh * HN:(gh + 1) * HN]
                dBu_out = dBu_pp[pb][:].rearrange("p (dc n t) -> p dc n t", dc=DG, n=HN)
                dtu_b = dtu_pp[dgi][:].rearrange("p (dc t) -> p dc t", dc=DG)[:, :, None, :].broadcast_to([128, DG, HN, T])
                bsl_b = bsl[:, None, :, :].broadcast_to([128, DG, HN, T])
                p.step("vector",
                       (lambda dBu_out=dBu_out, dtu_b=dtu_b, bsl_b=bsl_b: lambda v: v.tensor_tensor(
                           dBu_out, dtu_b, bsl_b, OP.mult))(),
                       waits=[(s_bc, bcd)], inc=(s_v, 1))
                dBu_pd[(gg, gh)] = s_v.n

            if tau in prol:
                lnd_g[0] = prol.pop(tau)
            else:
                emit_dt_s(0)
            emit_dA(0, 0)
            emit_dt_v(0)
            emit_dbu(0, 0)
            for g in range(NDG):
                dgi = g % 2
                dc0 = g * DG
                if g + 1 < NDG:
                    emit_dt_s(g + 1)
                pend = [None]

                def emit_tree(ent):
                    buf, pb, gh, cm_done, cm_eng = ent
                    gA = dA_pp[buf][:].rearrange("p (dc n t) -> p dc n t", dc=DG, n=HN)
                    l1 = dBu_pp[pb][:, 0:SW // 2].rearrange("p (dc n t) -> p dc n t", dc=DG, n=HN // 2)
                    slot = PTA if gh % 2 == 0 else PTB
                    tw = [(s_p, cm_done)] if cm_eng == "p" else []
                    p.step("vector",
                           (lambda gA=gA, l1=l1: lambda v: v.tensor_tensor(
                               l1, gA[:, :, 0:HN // 2], gA[:, :, HN // 2:HN], OP.add))(),
                           waits=tw, inc=(s_v, 1))
                    dA_free[buf] = s_v.n
                    p.step("vector",
                           (lambda l1=l1, slot=slot: lambda v: v.tensor_tensor(
                               part[:, slot:slot + DG * T].rearrange("p (dc t) -> p dc t", dc=DG),
                               l1[:, :, 0:1, :].rearrange("p dc n t -> p dc (n t)"),
                               l1[:, :, 1:2, :].rearrange("p dc n t -> p dc (n t)"),
                               OP.add))(),
                           inc=(s_v, 1))
                    dBu_free[pb] = s_v.n
                    if gh == 1:
                        p.step("vector", lambda v: v.tensor_tensor(
                            part[:, PTC:PTC + DG * T], part[:, PTA:PTA + DG * T], part[:, PTB:PTB + DG * T], OP.add))
                    elif gh == 3:
                        p.step("vector", lambda v: v.tensor_tensor(
                            part[:, PTA:PTA + DG * T], part[:, PTA:PTA + DG * T], part[:, PTB:PTB + DG * T], OP.add))

                for gh in range(NGH):
                    buf = (g * NGH + gh) % 2
                    pb = gh % 2
                    sts4 = state[:].rearrange("p (dc n) -> p dc n", dc=DC)[:, dc0:dc0 + DG, gh * HN:(gh + 1) * HN][:, :, :, None]
                    dA_fc = dA_pp[buf][:].rearrange("p (dc n t) -> p dc n t", dc=DG, n=HN)[:, :, :, 0:1]
                    dBu_fc = dBu_pp[pb][:].rearrange("p (dc n t) -> p dc n t", dc=DG, n=HN)[:, :, :, 0:1]
                    c32_4 = c32[:].rearrange("p (dc n) -> p dc n", dc=DG)[:, :, :, None]
                    p.step("vector",
                           (lambda dA_fc=dA_fc, sts4=sts4, c32_4=c32_4: lambda v: v.tensor_tensor(c32_4, dA_fc, sts4, OP.mult))(),
                           waits=[(s_s, dA_done[(g, gh)])])
                    p.step("vector",
                           (lambda buf=buf: lambda v: v.memset(
                               dA_pp[buf][:].rearrange("p (s t) -> p s t", s=DG * HN)[:, :, 0:1], 0.0))())
                    p.step("vector",
                           (lambda pb=pb: lambda v: v.tensor_scalar(
                               dBu_pp[pb][:], dBu_pp[pb][:], -10.0, 10.0, OP.max, OP.min))())
                    p.step("vector",
                           (lambda dBu_fc=dBu_fc, c32_4=c32_4: lambda v: v.tensor_tensor(dBu_fc, dBu_fc, c32_4, OP.add))())
                    vspace()
                    hb = (g * NGH + gh) % 2
                    p.step("vector",
                           (lambda buf=buf, pb=pb, hb=hb: lambda v: v.tensor_tensor_scan(
                               h_pp[hb][:], dA_pp[buf][:], dBu_pp[pb][:], 0.0, OP.mult, OP.add))(),
                           inc=(s_v, 1))
                    scan_done = s_v.n
                    if gh == 0 and g + 1 < NDG:
                        emit_dt_v(g + 1)
                    had_tree = pend[0] is not None
                    if had_tree:
                        emit_tree(pend[0])
                        pend[0] = None
                    nxt = (g, gh + 1) if gh + 1 < NGH else ((g + 1, 0) if g + 1 < NDG else None)
                    if nxt is not None:
                        emit_dA(*nxt)
                        emit_dbu(*nxt)
                    csl = crep[par][:].rearrange("p (n t) -> p n t", n=NSTATE)[:, gh * HN:(gh + 1) * HN]
                    csl_b = csl[:, None, :, :].broadcast_to([128, DG, HN, T])
                    h4 = h_pp[hb][:].rearrange("p (dc n t) -> p dc n t", dc=DG, n=HN)
                    gP = dA_pp[buf][:].rearrange("p (dc n t) -> p dc n t", dc=DG, n=HN)
                    p.step("vector",
                           (lambda gP=gP, h4=h4, csl_b=csl_b: lambda v: v.tensor_tensor(
                               gP, h4, csl_b, OP.mult))(),
                           waits=[(s_cc, ccd)], inc=(s_v, 1))
                    cm_done, cm_eng = s_v.n, "v"
                    h_lc = h_pp[hb][:].rearrange("p (dc n t) -> p dc n t", dc=DG, n=HN)[:, :, :, T - 1:T]
                    p.step("vector",
                           (lambda h_lc=h_lc, sts4=sts4: lambda v: v.tensor_copy(sts4, h_lc))())
                    pend[0] = (buf, pb, gh, cm_done, cm_eng)
                    if gh == 1:
                        if inter is not None:
                            inter(2 * g)
                        if pend_stats[0] is not None:
                            pend_stats[0]()
                            pend_stats[0] = None
                emit_tree(pend[0])
                pend[0] = None
                p.step("vector",
                       (lambda dc0=dc0: lambda v: v.tensor_tensor(
                           y_sb[:, dc0 * T:(dc0 + DG) * T], part[:, PTA:PTA + DG * T], part[:, PTC:PTC + DG * T], OP.add))(),
                       waits=[(s_t, y_free_t)], inc=(s_v, 1))
                y_done = s_v.n
                p.step("scalar",
                       (lambda dc0=dc0: lambda s: s.activation(
                           out=sqy[:], in_=y_sb[:, dc0 * T:(dc0 + DG) * T], func=AF.Square))(),
                       waits=[(s_v, y_done), (s_t, sq_t_mark)], inc=(s_s, 1))
                sqd = s_s.n
                if inter is not None:
                    inter(2 * g + 1)

                def emit_stats(g=g, dc0=dc0, y_done=y_done, sqd=sqd):
                    nonlocal sq_t_mark
                    for i in range(DG):
                        dc = dc0 + i
                        p.step("tensor",
                               (lambda dc=dc: lambda t: t.matmul(
                                   ps_stat[0:1, 0:T], ones_c, y_sb[:, dc * T:(dc + 1) * T],
                                   start=(dc == 0), stop=(dc == DC - 1)))(),
                               waits=[(s_v, max(y_done, pool_evac_v)), (s_s, stat_free_s)])
                        p.step("tensor",
                               (lambda i=i, dc=dc: lambda t: t.matmul(
                                   ps_stat[32:33, 0:T], ones_c, sqy[:, i * T:(i + 1) * T],
                                   start=(dc == 0), stop=(dc == DC - 1)))(),
                               waits=[(s_s, sqd)],
                               inc=((s_t, 1) if i == DG - 1 else None))
                    sq_t_mark = s_t.n

                pend_stats[0] = emit_stats
            if pend_stats[0] is not None:
                pend_stats[0]()
                pend_stats[0] = None
            stats_done_t = s_t.n
            p.step("scalar", lambda s: s.activation(out=stat_sb[0:1, :], in_=ps_stat[0:1, 0:T],
                                                    func=AF.Copy, scale=rden_col[0:1, :]),
                   waits=[(s_t, stats_done_t)], inc=(s_s, 1))
            p.step("scalar", lambda s: s.activation(out=stat_sb[32:33, :], in_=ps_stat[32:33, 0:T],
                                                    func=AF.Copy, scale=rden_col[32:33, :]),
                   inc=(s_s, 1))
            fin_s = s_s.n
            stat_free_s = fin_s
            if inter is not None:
                inter(8)
            p.step("sync", dma(d_scr2[0:1, :], stat_sb[0:1, :]), waits=[(s_s, fin_s)], inc=(s_sc2, 16))
            p.step("sync", dma(d_scr2[1:2, :], stat_sb[32:33, :]), inc=(s_sc2, 16))
            sc2w = s_sc2.n
            p.step("sync", lambda e: e.dma_start(
                out=m2r_rep[:], in_=d_scr2[:].rearrange("a b -> (a b)")[None, :].broadcast_to([128, 2 * T])),
                   waits=[(s_sc2, sc2w)], inc=(s_m2r, 16))
            m2rd = s_m2r.n
            p.step("vector", lambda v: v.tensor_tensor(sqy[:, 0:T], m2r_rep[:, 0:T], m2r_rep[:, 0:T], OP.mult),
                   waits=[(s_m2r, m2rd)])
            p.step("vector", lambda v: v.tensor_tensor(sqy[:, T:2 * T], m2r_rep[:, T:2 * T], sqy[:, 0:T], OP.subtract),
                   inc=(s_v, 1))
            var2d = s_v.n
            p.step("scalar", lambda s: s.activation(out=sqy[:, 2 * T:3 * T], in_=sqy[:, T:2 * T],
                                                    func=AF.Sqrt, bias=eps_col),
                   waits=[(s_v, var2d)], inc=(s_s, 1))
            sd2d = s_s.n
            p.step("vector", lambda v: v.reciprocal(rstd2_rep[:], sqy[:, 2 * T:3 * T]),
                   waits=[(s_s, sd2d)])
            vspace()
            mu_b = m2r_rep[:, 0:T][:, None, :].broadcast_to([128, DG, T])
            rs_b = rstd2_rep[:][:, None, :].broadcast_to([128, DG, T])
            for g in range(NDG):
                dc0 = g * DG
                ysl = y_sb[:, dc0 * T:(dc0 + DG) * T]
                p.step("vector",
                       (lambda ysl=ysl, mu_b=mu_b: lambda v: v.tensor_tensor(
                           sqy[:].rearrange("p (dc t) -> p dc t", dc=DG),
                           ysl.rearrange("p (dc t) -> p dc t", dc=DG), mu_b, OP.subtract))())
                p.step("vector",
                       (lambda rs_b=rs_b: lambda v: v.tensor_tensor(
                           dtmp[:].rearrange("p (dc t) -> p dc t", dc=DG),
                           sqy[:].rearrange("p (dc t) -> p dc t", dc=DG), rs_b, OP.mult))())
                for i in range(DG):
                    dc = dc0 + i
                    p.step("vector",
                           (lambda i=i, dc=dc: lambda v: v.tensor_scalar(
                               sqy[:, i * T:(i + 1) * T], dtmp[:, i * T:(i + 1) * T],
                               col_g(dc), col_b(dc), OP.mult, OP.add))())
                    p.step("vector",
                           (lambda i=i, dc=dc, par=par: lambda v: v.scalar_tensor_tensor(
                               dtmp[:, i * T:(i + 1) * T], u_sb[par][:, dc * T:(dc + 1) * T], col_D(dc),
                               sqy[:, i * T:(i + 1) * T], OP.mult, OP.add))())
                p.step("vector",
                       (lambda ysl=ysl, dc0=dc0, par=par: lambda v: v.tensor_tensor(
                           ysl, dtmp[:], sz_sb[par][:, dc0 * T:(dc0 + DG) * T], OP.mult))(),
                       inc=(s_v, 1))
            y3_done = s_v.n
            u_free_v[par] = y3_done
            sz_free_v[par] = y3_done
            brep_free_p[par] = y3_done
            crep_free_p[par] = y3_done
            dtr_free_s[par] = s_s.n
            prev_y3_v = y3_done
            return y3_done

        cout = {}
        cev = {}
        outps_free_v = [0]

        def phase_C_mm(tau, y3_done):
            nonlocal y_free_t
            par = tau % 2
            for dc in range(DC):
                q = dc % NWB
                p.step("gpsimd", (lambda q=q, dc=dc: lambda e: e.dma_start(
                    out=wbuf[q], in_=d_woutT[dc * 128:(dc + 1) * 128, :]))(),
                       waits=[(s_t, wfree[q])], inc=(s_wc[q], 16))
                wcmark = s_wc[q].n
                for m in range(8):
                    p.step("tensor",
                           (lambda q=q, dc=dc, m=m: lambda t: t.matmul(
                               ps_out[:, m * 128:(m + 1) * 128], wbuf[q][:, m * 128:(m + 1) * 128],
                               y_sb[:, dc * T:(dc + 1) * T],
                               start=(dc == 0 and m % 4 == 0), stop=False))(),
                           waits=[(s_wc[q], wcmark), (s_v, max(y3_done, outps_free_v[0]))],
                           inc=((s_t, 1) if m == 7 else None))
                wfree[q] = s_t.n
            for dm in range(DM):
                q = dm % NWB
                p.step("gpsimd", (lambda q=q, dm=dm: lambda e: e.dma_start(
                    out=wbuf[q], in_=d_ffnT[dm * 128:(dm + 1) * 128, :]))(),
                       waits=[(s_t, wfree[q])], inc=(s_wc[q], 16))
                wcmark = s_wc[q].n
                for m in range(8):
                    p.step("tensor",
                           (lambda q=q, dm=dm, m=m, par=par: lambda t: t.matmul(
                               ps_out[:, m * 128:(m + 1) * 128], wbuf[q][:, m * 128:(m + 1) * 128],
                               xn_fm[par][:, dm * T:(dm + 1) * T],
                               start=False, stop=False))(),
                           waits=[(s_wc[q], wcmark)],
                           inc=((s_t, 1) if m == 7 else None))
                wfree[q] = s_t.n
            xn_free_t[par] = s_t.n
            for m in range(8):
                p.step("tensor",
                       (lambda m=m, par=par: lambda t: t.matmul(
                           ps_out[:, m * 128:(m + 1) * 128], disha_sb[:, m * 128:(m + 1) * 128],
                           pooled_sb[par][:],
                           start=False, stop=(m % 4 == 3)))(),
                       inc=((s_t, 1) if m == 7 else None))
            out_mm = s_t.n
            y_free_t = out_mm
            pool_free_t[par] = out_mm
            cout[tau] = out_mm

        def phase_C_evac(tau):
            out_mm = cout.pop(tau)
            p.step("vector", lambda v: v.tensor_copy(fmout[:], ps_out[:]),
                   waits=[(s_t, out_mm)], inc=(s_v, 1))
            cev[tau] = s_v.n
            outps_free_v[0] = cev[tau]

        def phase_C_rest(tau, marks):
            nonlocal store_done
            par = tau % 2
            t0 = tau * T
            ev = cev.pop(tau)
            for m in range(8):
                bi = m % 2
                te, tm = tr_last[bi]
                tw = [(s_v, ev)] + ([(s_s, tm)] if te == "s" else [(s_v, tm)])
                p.step("tensor",
                       (lambda m=m, bi=bi: lambda t: t.transpose(
                           ps_tr[bi][:, 0:128], fmout[:, m * 128:(m + 1) * 128], ident))(),
                       waits=tw, inc=(s_t, 1))
                trd = s_t.n
                p.step("vector",
                       (lambda m=m, bi=bi, par3=tau % 3: lambda v: v.tensor_tensor(
                           out_tok[:, m * 128:(m + 1) * 128], ps_tr[bi][0:128, 0:128],
                           xstage[par3][:, m * 128:(m + 1) * 128], OP.add))(),
                       waits=[(s_t, trd), (s_x, marks["xw"]), (s_st, store_done)],
                       inc=(s_v, 1))
                tr_last[bi] = ("v", s_v.n)
            adds_done = s_v.n
            x_free_v[tau % 3] = adds_done
            p.step("sync", (lambda t0=t0: lambda e: e.dma_start(
                out=d_out[t0:t0 + T, :], in_=out_tok[:]))(),
                   waits=[(s_v, adds_done)], inc=(s_st, 16))
            store_done = s_st.n


        marks = {}
        y3s = {}
        st0 = phase_A_head(0)
        phase_A_pool(st0)
        for e in range(4):
            phase_A_eg_mm(0, e, st0)
            phase_A_eg_ev(0, e, st0)
        marks[0] = phase_A_tail(0, st0)
        for tau in range(NT):
            pieces = []
            if tau >= 1:
                phase_C_mm(tau - 1, y3s[tau - 1])
            if tau + 1 < NT:
                cell = {}
                pieces.append((lambda cell=cell, t2=tau + 1: (lambda: cell.__setitem__("st", phase_A_head(t2))))())
                pieces.append((lambda cell=cell, t2=tau + 1: (lambda: phase_A_eg_mm(t2, 0, cell["st"])))())
            if tau >= 1:
                pieces.append((lambda t1=tau - 1: (lambda: phase_C_evac(t1)))())
            if tau + 1 < NT:
                pieces.append((lambda cell=cell: (lambda: phase_A_pool(cell["st"])))())
                pieces.append((lambda cell=cell, t2=tau + 1: (lambda: phase_A_eg_ev(t2, 0, cell["st"])))())
                pieces.append((lambda cell=cell, t2=tau + 1: (lambda: phase_A_eg_mm(t2, 1, cell["st"])))())
            if tau >= 1:
                pieces.append((lambda t1=tau - 1: (lambda: phase_C_rest(t1, marks[t1])))())
            if tau + 1 < NT:
                for e in range(1, 4):
                    pieces.append((lambda e=e, cell=cell, t2=tau + 1: (lambda: phase_A_eg_ev(t2, e, cell["st"])))())
                    if e + 1 < 4:
                        pieces.append((lambda e=e, cell=cell, t2=tau + 1: (lambda: phase_A_eg_mm(t2, e + 1, cell["st"])))())
                pieces.append((lambda cell=cell, t2=tau + 1: (lambda: marks.__setitem__(t2, phase_A_tail(t2, cell["st"])))) ())
                pieces.append((lambda t2=tau + 1: (lambda: phase_B_prol(t2, marks[t2])))())

            TAKES = [2, 2, 2, 3, 2, 1, 1, 0, 99]

            def inter(k, pieces=pieces, TAKES=TAKES):
                for _ in range(TAKES[k]):
                    if pieces:
                        pieces.pop(0)()

            y3s[tau] = phase_B(tau, marks[tau], inter)
        phase_C_mm(NT - 1, y3s[NT - 1])
        phase_C_evac(NT - 1)
        phase_C_rest(NT - 1, marks[NT - 1])

        final_store = s_st.n
        p.step("sync", lambda e: e.nop(), waits=[(s_st, final_store)])

        with nc.Block() as block:
            @block.sync
            def _(sync):
                p.emit("sync", sync)

            @block.vector
            def _(vector):
                p.emit("vector", vector)

            @block.scalar
            def _(scalar):
                p.emit("scalar", scalar)

            @block.tensor
            def _(tensor):
                p.emit("tensor", tensor)

            @block.gpsimd
            def _(gpsimd):
                p.emit("gpsimd", gpsimd)

    return nc


def _host_layout(inputs):
    W_in = np.ascontiguousarray(inputs["L0_attn_W_in"])
    W_out = np.ascontiguousarray(inputs["L0_attn_W_out"])
    ffn_w = np.ascontiguousarray(inputs["L0_ffn_weight"])
    winT = np.ascontiguousarray(W_in.T.astype(np.float32))
    woutT = np.ascontiguousarray(W_out.T.astype(np.float32))
    ffnT = np.ascontiguousarray(ffn_w.T.astype(np.float32))
    disha = np.ascontiguousarray(inputs["L0_ffn_disha"].astype(np.float32))

    W_x = inputs["L0_attn_W_x"]
    wx = np.ascontiguousarray(
        W_x.T.reshape(DC, 128, 33).transpose(1, 0, 2).reshape(128, DC * 33).astype(np.float32))

    def chanv(a):
        return a.reshape(DC, 128).T

    chan = np.zeros((128, 176), np.float32)
    chan[:, 0:16] = chanv(inputs["L0_attn_W_dt"][:, 0])
    chan[:, 16:32] = chanv(inputs["L0_attn_b_dt"])
    chan[:, 32:48] = chanv(inputs["L0_attn_conv_b"])
    chan[:, 48:64] = chanv(inputs["L0_attn_D_param"])
    chan[:, 64:80] = chanv(inputs["L0_attn_ln_g"])
    chan[:, 80:96] = chanv(inputs["L0_attn_ln_b"])
    cw = inputs["L0_attn_conv_w"][:, 0, :]
    chan[:, 96:160] = cw.reshape(DC, 128, 4).transpose(1, 0, 2).reshape(128, 64)
    chan[:, 160:176] = np.broadcast_to(inputs["L0_attn_A_log"], (128, 16))

    cons = np.zeros((128, 195), np.float32)
    cons[:, 0:64] = np.tile(np.eye(64, dtype=np.float32), (2, 1))
    cons[:, 64] = 1.0
    cons[:, 65:193] = np.eye(128, dtype=np.float32)
    cons[:, 193] = LN_EPS
    cons[:, 194] = 1.0 / DIN

    return dict(winT=winT, woutT=woutT, ffnT=ffnT, disha=disha, wx=wx,
                chan=chan, cons=cons)


def kernel(**inputs):
    shared = _host_layout(inputs)
    x = np.ascontiguousarray(inputs["L0_attn_x"].astype(np.float32))
    nc = build_program()
    in_maps = []
    for b in range(8):
        m = dict(shared)
        m["x"] = np.ascontiguousarray(x[b])
        in_maps.append(m)
    res = run_bass_kernel_spmd(nc, in_maps, core_ids=list(range(8)))
    out = np.stack([r["out"] for r in res.results], axis=0)
    return out.astype(np.float32)


# version 24 (final)
# speedup vs baseline: 1.0424x; 1.0137x over previous
import numpy as np
from contextlib import ExitStack

import concourse.bass as bass
import concourse.mybir as mybir
from concourse.bass_utils import run_bass_kernel_spmd

F32 = mybir.dt.float32
AF = mybir.ActivationFunctionType
OP = mybir.AluOpType
AX = mybir.AxisListType.X

L = 2048
DMODEL = 1024
DIN = 2048
NSTATE = 16
T = 128
NT = L // T
DC = DIN // 128
DM = DMODEL // 128
DG = 4
NDG = DC // DG
HN = 4
NGH = NSTATE // HN
SW = DG * HN * T
LN_EPS = 1e-5


class Sem:
    def __init__(self, h):
        self.h = h
        self.n = 0


class Prog:
    def __init__(self):
        self.steps = []

    def step(self, engine, fn, waits=(), inc=None):
        for s, v in waits:
            assert v <= s.n, f"wait on future sem value {v} > {s.n}"
        self.steps.append((engine, fn, list(waits), inc))
        if inc is not None:
            inc[0].n += inc[1]

    def emit(self, engine_name, eng):
        last_wait = {}
        for e, fn, waits, inc in self.steps:
            if e != engine_name:
                continue
            for s, v in waits:
                if last_wait.get(id(s), -1) >= v:
                    continue
                eng.wait_ge(s.h, v)
                last_wait[id(s)] = v
            inst = fn(eng)
            if inc is not None:
                assert inst is not None
                inst.then_inc(inc[0].h, inc[1])


def build_program():
    nc = bass.Bass("TRN2", target_bir_lowering=False, debug=False,
                   detect_race_conditions=False)

    d_x = nc.dram_tensor("x", [L, DMODEL], F32, kind="ExternalInput").ap()
    d_winT = nc.dram_tensor("winT", [DMODEL, 2 * DIN], F32, kind="ExternalInput").ap()
    d_woutT = nc.dram_tensor("woutT", [DIN, DMODEL], F32, kind="ExternalInput").ap()
    d_ffnT = nc.dram_tensor("ffnT", [DMODEL, DMODEL], F32, kind="ExternalInput").ap()
    d_disha = nc.dram_tensor("disha", [64, DMODEL], F32, kind="ExternalInput").ap()
    d_wx = nc.dram_tensor("wx", [128, DC * 33], F32, kind="ExternalInput").ap()
    d_chan = nc.dram_tensor("chan", [128, 176], F32, kind="ExternalInput").ap()
    d_cons = nc.dram_tensor("cons", [128, 195], F32, kind="ExternalInput").ap()
    d_out = nc.dram_tensor("out", [L, DMODEL], F32, kind="ExternalOutput").ap()
    d_scr33 = nc.dram_tensor("scr33", [2, 33, T], F32, kind="Internal").ap()
    d_scr2 = nc.dram_tensor("scr2", [2, T], F32, kind="Internal").ap()

    p = Prog()

    with ExitStack() as ctx:
        sb = lambda name, shape: ctx.enter_context(nc.sbuf_tensor(name, shape, F32)).ap()
        ps = lambda name, shape: ctx.enter_context(nc.psum_tensor(name, shape, F32)).ap()

        wbuf = [sb(f"wbuf{i}", [128, 1024]) for i in range(3)]
        NWB = len(wbuf)
        wx_sb = sb("wx_sb", [128, DC * 33])
        disha_sb = sb("disha_sb", [64, DMODEL])
        chan_sb = sb("chan_sb", [128, 176])
        cons_sb = sb("cons_sb", [128, 195])
        a_tmp = sb("a_tmp", [128, NSTATE])
        a_neg = sb("a_neg", [128, NSTATE])

        xstage = [sb(f"xstage{i}", [128, 1024]) for i in range(3)]
        xn_tok = sb("xn_tok", [128, 1024])
        xn_fm = [sb(f"xn_fm{i}", [128, DM * T]) for i in range(2)]
        xc = sb("xc", [128, DC * (T + 3)])
        u_sb = [sb(f"u{i}", [128, DC * T]) for i in range(2)]
        sz_sb = [sb(f"sz{i}", [128, DC * T]) for i in range(2)]
        y_sb = sb("y", [128, DC * T])
        brep = [sb(f"brep{i}", [128, NSTATE * T]) for i in range(2)]
        crep = [sb(f"crep{i}", [128, NSTATE * T]) for i in range(2)]
        dtraw_rep = [sb(f"dtraw{i}", [128, T]) for i in range(2)]
        m2r_rep = sb("m2r", [128, 2 * T])
        rstd2_rep = sb("rstd2", [128, T])
        dt_g = [sb(f"dtg{i}", [128, DG * T]) for i in range(2)]
        dA_pp = [sb(f"dA{i}", [128, SW]) for i in range(2)]
        dBu_pp = [sb(f"dBu{i}", [128, SW]) for i in range(2)]
        h_pp = [sb(f"h{i}", [128, SW]) for i in range(2)]
        part = sb("part", [128, 3 * DG * T])
        c32 = sb("c32", [128, DG * HN])
        state = sb("state", [128, DC * NSTATE])
        xdbl_sb = sb("xdbl_sb", [33, T])
        pooled_sb = [sb(f"pooled{i}", [64, T]) for i in range(2)]
        sqy = sb("sqy", [128, DG * T])
        fmout = sb("fmout", [128, 1024])
        out_tok = fmout
        PTA, PTB, PTC = 0, DG * T, 2 * DG * T
        convacc = [sb(f"convacc{i}", [128, 2 * T]) for i in range(2)]
        dtmp = sb("dtmp", [128, DG * T])
        dts = sb("dts", [128, DG * T])
        dtu_pp = [sb(f"dtu{i}", [128, DG * T]) for i in range(2)]
        sums = sb("sums", [128, 8])
        mu_tok = sb("mu_tok", [128, 2])
        rstd_tok = sb("rstd_tok", [128, 2])
        spc = sb("spc", [128, 128])
        stat_sb = sb("stat_sb", [33, T])

        ps_xz = ps("ps_xz", [128, 1024])
        ps_out = ps("ps_out", [128, 1024])
        ps_misc = ps("ps_misc", [128, 512])
        ps_stat = ps("ps_stat", [128, 512])
        ps_tr = [ps("ps_tr0", [128, 512]), ps("ps_tr1", [128, 512])]

        s_ld = Sem(ctx.enter_context(nc.semaphore("s_ld")))
        s_st = Sem(ctx.enter_context(nc.semaphore("s_st")))
        s_v = Sem(ctx.enter_context(nc.semaphore("s_v")))
        s_s = Sem(ctx.enter_context(nc.semaphore("s_s")))
        s_t = Sem(ctx.enter_context(nc.semaphore("s_t")))
        s_p = Sem(ctx.enter_context(nc.semaphore("s_p")))
        s_wb = [Sem(ctx.enter_context(nc.semaphore(f"s_wb{i}"))) for i in range(NWB)]
        s_wc = [Sem(ctx.enter_context(nc.semaphore(f"s_wc{i}"))) for i in range(NWB)]
        s_x = Sem(ctx.enter_context(nc.semaphore("s_x")))
        s_scr33 = Sem(ctx.enter_context(nc.semaphore("s_scr33")))
        s_dtr = Sem(ctx.enter_context(nc.semaphore("s_dtr")))
        s_bc = Sem(ctx.enter_context(nc.semaphore("s_bc")))
        s_cc = Sem(ctx.enter_context(nc.semaphore("s_cc")))
        s_sc2 = Sem(ctx.enter_context(nc.semaphore("s_sc2")))
        s_m2r = Sem(ctx.enter_context(nc.semaphore("s_m2r")))

        col_wdt = lambda dc: chan_sb[:, dc:dc + 1]
        col_bdt = lambda dc: chan_sb[:, 16 + dc:16 + dc + 1]
        col_cb = lambda dc: chan_sb[:, 32 + dc:32 + dc + 1]
        col_D = lambda dc: chan_sb[:, 48 + dc:48 + dc + 1]
        col_g = lambda dc: chan_sb[:, 64 + dc:64 + dc + 1]
        col_b = lambda dc: chan_sb[:, 80 + dc:80 + dc + 1]
        col_cw = lambda dc, k: chan_sb[:, 96 + dc * 4 + k:96 + dc * 4 + k + 1]
        alog = chan_sb[:, 160:176]
        mpool = cons_sb[:, 0:64]
        ones_c = cons_sb[:, 64:65]
        ident = cons_sb[:, 65:193]
        eps_col = cons_sb[:, 193:194]
        rden_col = cons_sb[:, 194:195]

        def dma(dst, src):
            return lambda eng: eng.dma_start(out=dst, in_=src)

        def vspace():
            p.step("vector", lambda v: v.tensor_copy(spc[:], cons_sb[:, 65:193]))

        wmark = [0] * NWB
        wfree = [0] * NWB
        tr_last = [("v", 0), ("v", 0)]
        u_free_v = [0, 0]
        sz_free_v = [0, 0]
        xn_free_t = [0, 0]
        x_free_v = [0, 0, 0]
        pool_free_t = [0, 0]
        brep_free_p = [0, 0]
        dtr_free_s = [0, 0]
        dA_free = [0, 0]
        dBu_free = [0, 0]
        crep_free_p = [0, 0]
        y_free_t = 0
        store_done = 0
        xz_evac_mark = 0
        misc_free_s = 0
        stat_free_s = 0
        pool_evac_v = 0
        dtu_vmark = [0, 0]
        xn_tok_free_t = 0
        prev_y3_v = 0
        last_ctr_t = 0
        prev_u_done = [0, 0]

        p.step("sync", dma(wx_sb, d_wx), inc=(s_ld, 16))
        p.step("sync", dma(disha_sb, d_disha), inc=(s_ld, 16))
        p.step("sync", dma(chan_sb, d_chan), inc=(s_ld, 16))
        p.step("sync", dma(cons_sb, d_cons), inc=(s_ld, 16))
        ld_setup = s_ld.n
        p.step("vector", lambda v: v.memset(state[:], 0.0), waits=[(s_ld, ld_setup)])
        p.step("vector", lambda v: v.memset(xc[:], 0.0))
        p.step("vector", lambda v: v.tensor_scalar_min(a_tmp[:], alog, 5.0), inc=(s_v, 1))
        va = s_v.n
        p.step("scalar", lambda s: s.activation(out=a_neg[:], in_=a_tmp[:], func=AF.Exp),
               waits=[(s_v, va)], inc=(s_s, 1))
        sa = s_s.n
        p.step("vector", lambda v: v.tensor_scalar_mul(a_tmp[:], a_neg[:], -1.0),
               waits=[(s_s, sa)], inc=(s_v, 1))


        def phase_A_head(tau):
            nonlocal pool_evac_v, xn_tok_free_t
            par = tau % 2
            t0 = tau * T
            par3 = tau % 3
            st = dict(par=par, misc_free_at_entry=misc_free_s)
            p.step("sync", dma(xstage[par3][:], d_x[t0:t0 + T, :]),
                   waits=[(s_v, x_free_v[par3])], inc=(s_x, 16))
            xw = s_x.n
            for dc in range(DC):
                o = dc * (T + 3)
                p.step("vector",
                       (lambda o=o: lambda v: v.tensor_copy(xc[:, o:o + 3], xc[:, o + T:o + T + 3]))(),
                       inc=((s_v, 1) if dc == DC - 1 else None))
            halo_done = s_v.n
            p.step("vector", (lambda par3=par3: lambda v: v.reduce_sum(sums[:, 0:1], xstage[par3][:], AX))(),
                   waits=[(s_x, xw)])
            p.step("scalar",
                   (lambda par=par, par3=par3: lambda s: s.activation(out=u_sb[par][:, 0:1024], in_=xstage[par3][:], func=AF.Square))(),
                   waits=[(s_x, xw), (s_v, u_free_v[par])], inc=(s_s, 1))
            sqd = s_s.n
            p.step("vector",
                   (lambda par=par: lambda v: v.reduce_sum(sums[:, 1:2], u_sb[par][:, 0:1024], AX))(),
                   waits=[(s_s, sqd)])
            p.step("vector", lambda v: v.tensor_scalar_mul(mu_tok[:, 0:1], sums[:, 0:1], 1.0 / DMODEL))
            vspace()
            p.step("vector", lambda v: v.tensor_tensor(sums[:, 2:3], mu_tok[:, 0:1], mu_tok[:, 0:1], OP.mult))
            p.step("vector", lambda v: v.tensor_scalar_mul(sums[:, 3:4], sums[:, 1:2], 1.0 / DMODEL))
            vspace()
            p.step("vector", lambda v: v.tensor_tensor(sums[:, 3:4], sums[:, 3:4], sums[:, 2:3], OP.subtract),
                   inc=(s_v, 1))
            var_done = s_v.n
            p.step("scalar", lambda s: s.activation(out=sums[:, 2:3], in_=sums[:, 3:4], func=AF.Sqrt, bias=eps_col),
                   waits=[(s_v, var_done)], inc=(s_s, 1))
            sd_done = s_s.n
            p.step("vector", lambda v: v.reciprocal(rstd_tok[:, 0:1], sums[:, 2:3]),
                   waits=[(s_s, sd_done)])
            vspace()
            p.step("vector",
                   (lambda par3=par3: lambda v: v.tensor_scalar(
                       xn_tok[:], xstage[par3][:], mu_tok[:, 0:1], rstd_tok[:, 0:1],
                       OP.subtract, OP.mult))(),
                   waits=[(s_t, xn_tok_free_t)], inc=(s_v, 1))
            xn_done = s_v.n
            for dm in range(DM):
                bi = dm % 2
                te, tm = tr_last[bi]
                tw = [(s_v, xn_done)] + ([(s_s, tm)] if te == "s" else [(s_v, tm)])
                p.step("tensor",
                       (lambda dm=dm, bi=bi: lambda t: t.transpose(
                           ps_tr[bi][:, 0:128], xn_tok[:, dm * 128:(dm + 1) * 128], ident))(),
                       waits=tw, inc=(s_t, 1))
                trd = s_t.n
                p.step("scalar",
                       (lambda dm=dm, bi=bi, par=par: lambda s: s.copy(
                           xn_fm[par][:, dm * T:(dm + 1) * T], ps_tr[bi][:, 0:128]))(),
                       waits=[(s_t, trd), (s_t, xn_free_t[par])], inc=(s_s, 1))
                tr_last[bi] = ("s", s_s.n)
            xn_tok_free_t = s_t.n
            xn_fm_done = s_s.n
            st.update(xw=xw, halo_done=halo_done, xn_fm_done=xn_fm_done)
            return st

        def phase_A_pool(st):
            nonlocal pool_evac_v
            par = st["par"]
            xn_fm_done = st["xn_fm_done"]
            for dm in range(DM):
                p.step("tensor",
                       (lambda dm=dm, par=par: lambda t: t.matmul(
                           ps_stat[0:64, T:2 * T], mpool, xn_fm[par][:, dm * T:(dm + 1) * T],
                           start=(dm == 0), stop=(dm == DM - 1)))(),
                       waits=[(s_s, max(xn_fm_done, stat_free_s))],
                       inc=((s_t, 1) if dm == DM - 1 else None))
            pool_done_t = s_t.n
            p.step("vector",
                   (lambda par=par: lambda v: v.tensor_copy(pooled_sb[par][:], ps_stat[0:64, T:2 * T]))(),
                   waits=[(s_t, pool_done_t), (s_t, pool_free_t[par])], inc=(s_v, 1))
            pool_evac_v = s_v.n

        def phase_A_eg_mm(tau, eg, st):
            nonlocal xz_evac_mark
            par = st["par"]
            xn_fm_done = st["xn_fm_done"]
            for dm in range(DM):
                q = (eg * DM + dm) % NWB
                p.step("sync", (lambda q=q, dm=dm, eg=eg: lambda e: e.dma_start(
                    out=wbuf[q],
                    in_=d_winT[dm * 128:(dm + 1) * 128, eg * 1024:(eg + 1) * 1024]))(),
                       waits=[(s_t, wfree[q])], inc=(s_wb[q], 16))
                wmark[q] = s_wb[q].n
                for e in range(8):
                    p.step("tensor",
                           (lambda q=q, dm=dm, e=e, par=par: lambda t: t.matmul(
                               ps_xz[:, e * T:(e + 1) * T],
                               wbuf[q][:, e * 128:(e + 1) * 128],
                               xn_fm[par][:, dm * T:(dm + 1) * T],
                               start=(dm == 0 and e % 4 == 0),
                               stop=(dm == DM - 1 and e % 4 == 3)))(),
                           waits=[(s_wb[q], wmark[q]), (s_s, max(xn_fm_done, xz_evac_mark))],
                           inc=((s_t, 1) if e == 7 else None))
                wfree[q] = s_t.n
            st[f"mm{eg}"] = s_t.n

        def phase_A_eg_ev(tau, eg, st):
            nonlocal xz_evac_mark, prev_u_done
            par = st["par"]
            halo_done = st["halo_done"]
            misc_free_at_entry = st["misc_free_at_entry"]
            mm_done = st[f"mm{eg}"]
            if True:
                if eg < 2:
                    dc0 = eg * 8
                    xc_out = xc[:].rearrange("p (dc w) -> p dc w", dc=DC)[:, dc0:dc0 + 8, 3:3 + T]
                    p.step("scalar",
                           (lambda xc_out=xc_out: lambda s: s.copy(
                               xc_out, ps_xz[:].rearrange("p (e t) -> p e t", e=8)))(),
                           waits=[(s_t, mm_done), (s_v, halo_done)], inc=(s_s, 1))
                    xz_evac_mark = s_s.n
                    xc_done = s_s.n
                    for dc in range(dc0, dc0 + 8):
                        o = dc * (T + 3)
                        ca = convacc[dc % 2]
                        p.step("vector",
                               (lambda o=o, dc=dc, ca=ca: lambda v: v.tensor_scalar(
                                   ca[:, 0:T], xc[:, o:o + T], col_cw(dc, 0), None, OP.mult))(),
                               waits=[(s_s, max(xc_done, prev_u_done[dc % 2]))])
                        for k in (1, 2, 3):
                            src = ca[:, 0:T] if k % 2 == 1 else ca[:, T:2 * T]
                            dst = ca[:, T:2 * T] if k % 2 == 1 else ca[:, 0:T]
                            p.step("vector",
                                   (lambda o=o, dc=dc, k=k, src=src, dst=dst: lambda v: v.scalar_tensor_tensor(
                                       dst, xc[:, o + k:o + k + T], col_cw(dc, k), src,
                                       OP.mult, OP.add))(),
                                   inc=((s_v, 1) if k == 3 else None))
                        conv_done = s_v.n
                        p.step("scalar",
                               (lambda dc=dc, par=par, ca=ca: lambda s: s.activation(
                                   out=u_sb[par][:, dc * T:(dc + 1) * T], in_=ca[:, T:2 * T],
                                   func=AF.Silu, bias=col_cb(dc)))(),
                               waits=[(s_v, conv_done)], inc=(s_s, 1))
                        u_done = s_s.n
                        prev_u_done[dc % 2] = u_done
                        p.step("tensor",
                               (lambda dc=dc, par=par: lambda t: t.matmul(
                                   ps_misc[0:33, 0:T], wx_sb[:, dc * 33:(dc + 1) * 33],
                                   u_sb[par][:, dc * T:(dc + 1) * T],
                                   start=(dc == 0), stop=(dc == DC - 1)))(),
                               waits=[(s_s, u_done if dc > 0 else max(u_done, misc_free_at_entry))],
                               inc=((s_t, 1) if dc == DC - 1 else None))
                else:
                    dcz0 = (eg - 2) * 8
                    p.step("scalar",
                           (lambda dcz0=dcz0, par=par: lambda s: s.activation(
                               out=sz_sb[par][:, dcz0 * T:(dcz0 + 8) * T], in_=ps_xz[:], func=AF.Silu))(),
                           waits=[(s_t, mm_done), (s_v, sz_free_v[par])], inc=(s_s, 1))
                    xz_evac_mark = s_s.n

        def phase_A_tail(tau, st):
            nonlocal misc_free_s
            par = st["par"]
            xw = st["xw"]
            xdbl_mm_done = s_t.n
            p.step("scalar", lambda s: s.copy(xdbl_sb[:], ps_misc[0:33, 0:T]),
                   waits=[(s_t, xdbl_mm_done)], inc=(s_s, 1))
            xdbl_done = s_s.n
            misc_free_s = xdbl_done
            p.step("gpsimd", (lambda tau=tau: lambda e: e.dma_start(
                out=d_scr33[tau % 2], in_=xdbl_sb[:]))(),
                   waits=[(s_s, xdbl_done)], inc=(s_scr33, 16))
            scr_w = s_scr33.n
            p.step("gpsimd", (lambda par=par, tau=tau: lambda e: e.dma_start(
                out=dtraw_rep[par][:], in_=d_scr33[tau % 2, 0:1, :].broadcast_to([128, T])))(),
                   waits=[(s_scr33, scr_w), (s_s, dtr_free_s[par])], inc=(s_dtr, 16))
            p.step("gpsimd", (lambda par=par, tau=tau: lambda e: e.dma_start(
                out=brep[par][:],
                in_=d_scr33[tau % 2, 1:17, :].rearrange("a b -> (a b)")[None, :].broadcast_to([128, NSTATE * T])))(),
                   waits=[(s_v, brep_free_p[par])], inc=(s_bc, 16))
            p.step("gpsimd", (lambda par=par, tau=tau: lambda e: e.dma_start(
                out=crep[par][:],
                in_=d_scr33[tau % 2, 17:33, :].rearrange("a b -> (a b)")[None, :].broadcast_to([128, NSTATE * T])))(),
                   waits=[(s_v, crep_free_p[par])], inc=(s_cc, 16))
            return dict(dtr=s_dtr.n, bc=s_bc.n, cc=s_cc.n, xw=xw, pool_evac=pool_evac_v)

        prol = {}

        def phase_B_prol(t2, mk):
            par = t2 % 2
            dtrd = mk["dtr"]
            for i in range(DG):
                p.step("scalar",
                       (lambda i=i, par=par: lambda s: s.activation(
                           out=dts[:, i * T:(i + 1) * T], in_=dtraw_rep[par][:], func=AF.Exp,
                           scale=col_wdt(i), bias=col_bdt(i)))(),
                       waits=([(s_dtr, dtrd)] if i == 0 else []))
            p.step("scalar", lambda s: s.activation(
                   out=dt_g[0][:], in_=dts[:], func=AF.Ln, bias=ones_c),
                   waits=[(s_v, dtu_vmark[0])], inc=(s_s, 1))
            lnd = s_s.n
            for j in range(HN):
                dA_out = dA_pp[0][:].rearrange("p (dc n t) -> p n dc t", dc=DG, n=HN)[:, j]
                p.step("scalar",
                       (lambda dA_out=dA_out, j=j: lambda s: s.activation(
                           out=dA_out, in_=dt_g[0][:], func=AF.Exp, scale=a_tmp[:, j:j + 1]))(),
                       waits=[(s_v, dA_free[0])],
                       inc=((s_s, 1) if j == HN - 1 else None))
            prol[t2] = (lnd, s_s.n)

        def phase_B(tau, marks, inter=None):
            nonlocal prev_y3_v, stat_free_s
            par = tau % 2
            dtrd, bcd, ccd = marks["dtr"], marks["bc"], marks["cc"]
            pend_stats = [None]
            sq_t_mark = 0

            lnd_g = {}

            def emit_dt_s(gg):
                dgi = gg % 2
                dc0 = gg * DG
                for i in range(DG):
                    dc = dc0 + i
                    p.step("scalar",
                           (lambda dc=dc, i=i, par=par: lambda s: s.activation(
                               out=dts[:, i * T:(i + 1) * T], in_=dtraw_rep[par][:], func=AF.Exp,
                               scale=col_wdt(dc), bias=col_bdt(dc)))(),
                           waits=([(s_dtr, dtrd)] if i == 0 else []))
                p.step("scalar", (lambda dgi=dgi: lambda s: s.activation(
                       out=dt_g[dgi][:], in_=dts[:], func=AF.Ln, bias=ones_c))(),
                       waits=[(s_v, dtu_vmark[dgi])], inc=(s_s, 1))
                lnd_g[gg] = s_s.n

            def emit_dt_v(gg):
                dgi = gg % 2
                dc0 = gg * DG
                p.step("vector",
                       (lambda dgi=dgi, dc0=dc0, par=par: lambda v: v.tensor_tensor(
                           dtu_pp[dgi][:], dt_g[dgi][:], u_sb[par][:, dc0 * T:(dc0 + DG) * T], OP.mult))(),
                       waits=[(s_s, lnd_g[gg])], inc=(s_v, 1))
                dtu_vmark[dgi] = s_v.n

            dA_done = {}
            dBu_pd = {}

            def emit_dA(gg, gh):
                buf = (gg * NGH + gh) % 2
                dgi = gg % 2
                for j in range(HN):
                    n = gh * HN + j
                    dA_out = dA_pp[buf][:].rearrange("p (dc n t) -> p n dc t", dc=DG, n=HN)[:, j]
                    p.step("scalar",
                           (lambda dA_out=dA_out, n=n, dgi=dgi: lambda s: s.activation(
                               out=dA_out, in_=dt_g[dgi][:], func=AF.Exp, scale=a_tmp[:, n:n + 1]))(),
                           waits=[(s_v, dA_free[buf])],
                           inc=((s_s, 1) if j == HN - 1 else None))
                dA_done[(gg, gh)] = s_s.n

            def emit_dbu(gg, gh):
                pb = gh % 2
                dgi = gg % 2
                bsl = brep[par][:].rearrange("p (n t) -> p n t", n=NSTATE)[:, gh * HN:(gh + 1) * HN]
                dBu_out = dBu_pp[pb][:].rearrange("p (dc n t) -> p dc n t", dc=DG, n=HN)
                dtu_b = dtu_pp[dgi][:].rearrange("p (dc t) -> p dc t", dc=DG)[:, :, None, :].broadcast_to([128, DG, HN, T])
                bsl_b = bsl[:, None, :, :].broadcast_to([128, DG, HN, T])
                p.step("vector",
                       (lambda dBu_out=dBu_out, dtu_b=dtu_b, bsl_b=bsl_b: lambda v: v.tensor_tensor(
                           dBu_out, dtu_b, bsl_b, OP.mult))(),
                       waits=[(s_bc, bcd)], inc=(s_v, 1))
                dBu_pd[(gg, gh)] = s_v.n

            if tau in prol:
                lnd_g[0], dA_done[(0, 0)] = prol.pop(tau)
            else:
                emit_dt_s(0)
                emit_dA(0, 0)
            emit_dt_v(0)
            emit_dbu(0, 0)
            for g in range(NDG):
                dgi = g % 2
                dc0 = g * DG
                if g + 1 < NDG:
                    emit_dt_s(g + 1)
                pend = [None]

                def emit_tree(ent):
                    buf, pb, gh, cm_done, cm_eng = ent
                    gA = dA_pp[buf][:].rearrange("p (dc n t) -> p dc n t", dc=DG, n=HN)
                    l1 = dBu_pp[pb][:, 0:SW // 2].rearrange("p (dc n t) -> p dc n t", dc=DG, n=HN // 2)
                    slot = PTA if gh % 2 == 0 else PTB
                    tw = [(s_p, cm_done)] if cm_eng == "p" else []
                    p.step("vector",
                           (lambda gA=gA, l1=l1: lambda v: v.tensor_tensor(
                               l1, gA[:, :, 0:HN // 2], gA[:, :, HN // 2:HN], OP.add))(),
                           waits=tw, inc=(s_v, 1))
                    dA_free[buf] = s_v.n
                    p.step("vector",
                           (lambda l1=l1, slot=slot: lambda v: v.tensor_tensor(
                               part[:, slot:slot + DG * T].rearrange("p (dc t) -> p dc t", dc=DG),
                               l1[:, :, 0:1, :].rearrange("p dc n t -> p dc (n t)"),
                               l1[:, :, 1:2, :].rearrange("p dc n t -> p dc (n t)"),
                               OP.add))(),
                           inc=(s_v, 1))
                    dBu_free[pb] = s_v.n
                    if gh == 1:
                        p.step("vector", lambda v: v.tensor_tensor(
                            part[:, PTC:PTC + DG * T], part[:, PTA:PTA + DG * T], part[:, PTB:PTB + DG * T], OP.add))
                    elif gh == 3:
                        p.step("vector", lambda v: v.tensor_tensor(
                            part[:, PTA:PTA + DG * T], part[:, PTA:PTA + DG * T], part[:, PTB:PTB + DG * T], OP.add))

                for gh in range(NGH):
                    buf = (g * NGH + gh) % 2
                    pb = gh % 2
                    sts4 = state[:].rearrange("p (dc n) -> p dc n", dc=DC)[:, dc0:dc0 + DG, gh * HN:(gh + 1) * HN][:, :, :, None]
                    dA_fc = dA_pp[buf][:].rearrange("p (dc n t) -> p dc n t", dc=DG, n=HN)[:, :, :, 0:1]
                    dBu_fc = dBu_pp[pb][:].rearrange("p (dc n t) -> p dc n t", dc=DG, n=HN)[:, :, :, 0:1]
                    c32_4 = c32[:].rearrange("p (dc n) -> p dc n", dc=DG)[:, :, :, None]
                    p.step("vector",
                           (lambda dA_fc=dA_fc, sts4=sts4, c32_4=c32_4: lambda v: v.tensor_tensor(c32_4, dA_fc, sts4, OP.mult))(),
                           waits=[(s_s, dA_done[(g, gh)])])
                    p.step("vector",
                           (lambda buf=buf: lambda v: v.memset(
                               dA_pp[buf][:].rearrange("p (s t) -> p s t", s=DG * HN)[:, :, 0:1], 0.0))())
                    p.step("vector",
                           (lambda pb=pb: lambda v: v.tensor_scalar(
                               dBu_pp[pb][:], dBu_pp[pb][:], -10.0, 10.0, OP.max, OP.min))())
                    p.step("vector",
                           (lambda dBu_fc=dBu_fc, c32_4=c32_4: lambda v: v.tensor_tensor(dBu_fc, dBu_fc, c32_4, OP.add))())
                    vspace()
                    hb = (g * NGH + gh) % 2
                    p.step("vector",
                           (lambda buf=buf, pb=pb, hb=hb: lambda v: v.tensor_tensor_scan(
                               h_pp[hb][:], dA_pp[buf][:], dBu_pp[pb][:], 0.0, OP.mult, OP.add))(),
                           inc=(s_v, 1))
                    scan_done = s_v.n
                    if gh == 0 and g + 1 < NDG:
                        emit_dt_v(g + 1)
                    had_tree = pend[0] is not None
                    if had_tree:
                        emit_tree(pend[0])
                        pend[0] = None
                    nxt = (g, gh + 1) if gh + 1 < NGH else ((g + 1, 0) if g + 1 < NDG else None)
                    if nxt is not None:
                        emit_dA(*nxt)
                        emit_dbu(*nxt)
                    csl = crep[par][:].rearrange("p (n t) -> p n t", n=NSTATE)[:, gh * HN:(gh + 1) * HN]
                    csl_b = csl[:, None, :, :].broadcast_to([128, DG, HN, T])
                    h4 = h_pp[hb][:].rearrange("p (dc n t) -> p dc n t", dc=DG, n=HN)
                    gP = dA_pp[buf][:].rearrange("p (dc n t) -> p dc n t", dc=DG, n=HN)
                    p.step("vector",
                           (lambda gP=gP, h4=h4, csl_b=csl_b: lambda v: v.tensor_tensor(
                               gP, h4, csl_b, OP.mult))(),
                           waits=[(s_cc, ccd)], inc=(s_v, 1))
                    cm_done, cm_eng = s_v.n, "v"
                    h_lc = h_pp[hb][:].rearrange("p (dc n t) -> p dc n t", dc=DG, n=HN)[:, :, :, T - 1:T]
                    p.step("vector",
                           (lambda h_lc=h_lc, sts4=sts4: lambda v: v.tensor_copy(sts4, h_lc))())
                    pend[0] = (buf, pb, gh, cm_done, cm_eng)
                    if gh == 1:
                        if inter is not None:
                            inter(2 * g)
                        if pend_stats[0] is not None:
                            pend_stats[0]()
                            pend_stats[0] = None
                emit_tree(pend[0])
                pend[0] = None
                p.step("vector",
                       (lambda dc0=dc0: lambda v: v.tensor_tensor(
                           y_sb[:, dc0 * T:(dc0 + DG) * T], part[:, PTA:PTA + DG * T], part[:, PTC:PTC + DG * T], OP.add))(),
                       waits=[(s_t, y_free_t)], inc=(s_v, 1))
                y_done = s_v.n
                p.step("scalar",
                       (lambda dc0=dc0: lambda s: s.activation(
                           out=sqy[:], in_=y_sb[:, dc0 * T:(dc0 + DG) * T], func=AF.Square))(),
                       waits=[(s_v, y_done), (s_t, sq_t_mark)], inc=(s_s, 1))
                sqd = s_s.n
                if inter is not None:
                    inter(2 * g + 1)

                def emit_stats(g=g, dc0=dc0, y_done=y_done, sqd=sqd):
                    nonlocal sq_t_mark
                    for i in range(DG):
                        dc = dc0 + i
                        p.step("tensor",
                               (lambda dc=dc: lambda t: t.matmul(
                                   ps_stat[0:1, 0:T], ones_c, y_sb[:, dc * T:(dc + 1) * T],
                                   start=(dc == 0), stop=(dc == DC - 1)))(),
                               waits=[(s_v, max(y_done, pool_evac_v)), (s_s, stat_free_s)])
                        p.step("tensor",
                               (lambda i=i, dc=dc: lambda t: t.matmul(
                                   ps_stat[32:33, 0:T], ones_c, sqy[:, i * T:(i + 1) * T],
                                   start=(dc == 0), stop=(dc == DC - 1)))(),
                               waits=[(s_s, sqd)],
                               inc=((s_t, 1) if i == DG - 1 else None))
                    sq_t_mark = s_t.n

                pend_stats[0] = emit_stats
            if pend_stats[0] is not None:
                pend_stats[0]()
                pend_stats[0] = None
            stats_done_t = s_t.n
            p.step("scalar", lambda s: s.activation(out=stat_sb[0:1, :], in_=ps_stat[0:1, 0:T],
                                                    func=AF.Copy, scale=rden_col[0:1, :]),
                   waits=[(s_t, stats_done_t)], inc=(s_s, 1))
            p.step("scalar", lambda s: s.activation(out=stat_sb[32:33, :], in_=ps_stat[32:33, 0:T],
                                                    func=AF.Copy, scale=rden_col[32:33, :]),
                   inc=(s_s, 1))
            fin_s = s_s.n
            stat_free_s = fin_s
            if inter is not None:
                inter(8)
            p.step("gpsimd", dma(d_scr2[0:1, :], stat_sb[0:1, :]), waits=[(s_s, fin_s)], inc=(s_sc2, 16))
            p.step("gpsimd", dma(d_scr2[1:2, :], stat_sb[32:33, :]), inc=(s_sc2, 16))
            sc2w = s_sc2.n
            p.step("gpsimd", lambda e: e.dma_start(
                out=m2r_rep[:], in_=d_scr2[:].rearrange("a b -> (a b)")[None, :].broadcast_to([128, 2 * T])),
                   waits=[(s_sc2, sc2w)], inc=(s_m2r, 16))
            m2rd = s_m2r.n
            p.step("vector", lambda v: v.tensor_tensor(sqy[:, 0:T], m2r_rep[:, 0:T], m2r_rep[:, 0:T], OP.mult),
                   waits=[(s_m2r, m2rd)])
            p.step("vector", lambda v: v.tensor_tensor(sqy[:, T:2 * T], m2r_rep[:, T:2 * T], sqy[:, 0:T], OP.subtract),
                   inc=(s_v, 1))
            var2d = s_v.n
            p.step("scalar", lambda s: s.activation(out=sqy[:, 2 * T:3 * T], in_=sqy[:, T:2 * T],
                                                    func=AF.Sqrt, bias=eps_col),
                   waits=[(s_v, var2d)], inc=(s_s, 1))
            sd2d = s_s.n
            p.step("vector", lambda v: v.reciprocal(rstd2_rep[:], sqy[:, 2 * T:3 * T]),
                   waits=[(s_s, sd2d)])
            vspace()
            mu_b = m2r_rep[:, 0:T][:, None, :].broadcast_to([128, DG, T])
            rs_b = rstd2_rep[:][:, None, :].broadcast_to([128, DG, T])
            for g in range(NDG):
                dc0 = g * DG
                ysl = y_sb[:, dc0 * T:(dc0 + DG) * T]
                p.step("vector",
                       (lambda ysl=ysl, mu_b=mu_b: lambda v: v.tensor_tensor(
                           sqy[:].rearrange("p (dc t) -> p dc t", dc=DG),
                           ysl.rearrange("p (dc t) -> p dc t", dc=DG), mu_b, OP.subtract))())
                p.step("vector",
                       (lambda rs_b=rs_b: lambda v: v.tensor_tensor(
                           dtmp[:].rearrange("p (dc t) -> p dc t", dc=DG),
                           sqy[:].rearrange("p (dc t) -> p dc t", dc=DG), rs_b, OP.mult))())
                for i in range(DG):
                    dc = dc0 + i
                    p.step("vector",
                           (lambda i=i, dc=dc: lambda v: v.tensor_scalar(
                               sqy[:, i * T:(i + 1) * T], dtmp[:, i * T:(i + 1) * T],
                               col_g(dc), col_b(dc), OP.mult, OP.add))())
                    p.step("vector",
                           (lambda i=i, dc=dc, par=par: lambda v: v.scalar_tensor_tensor(
                               dtmp[:, i * T:(i + 1) * T], u_sb[par][:, dc * T:(dc + 1) * T], col_D(dc),
                               sqy[:, i * T:(i + 1) * T], OP.mult, OP.add))())
                p.step("vector",
                       (lambda ysl=ysl, dc0=dc0, par=par: lambda v: v.tensor_tensor(
                           ysl, dtmp[:], sz_sb[par][:, dc0 * T:(dc0 + DG) * T], OP.mult))(),
                       inc=(s_v, 1))
            y3_done = s_v.n
            u_free_v[par] = y3_done
            sz_free_v[par] = y3_done
            brep_free_p[par] = y3_done
            crep_free_p[par] = y3_done
            dtr_free_s[par] = s_s.n
            prev_y3_v = y3_done
            return y3_done

        cout = {}
        cev = {}
        outps_free_v = [0]

        def phase_C_mm(tau, y3_done):
            nonlocal y_free_t
            par = tau % 2
            for dc in range(DC):
                q = dc % NWB
                p.step("gpsimd", (lambda q=q, dc=dc: lambda e: e.dma_start(
                    out=wbuf[q], in_=d_woutT[dc * 128:(dc + 1) * 128, :]))(),
                       waits=[(s_t, wfree[q])], inc=(s_wc[q], 16))
                wcmark = s_wc[q].n
                for m in range(8):
                    p.step("tensor",
                           (lambda q=q, dc=dc, m=m: lambda t: t.matmul(
                               ps_out[:, m * 128:(m + 1) * 128], wbuf[q][:, m * 128:(m + 1) * 128],
                               y_sb[:, dc * T:(dc + 1) * T],
                               start=(dc == 0 and m % 4 == 0), stop=False))(),
                           waits=[(s_wc[q], wcmark), (s_v, max(y3_done, outps_free_v[0]))],
                           inc=((s_t, 1) if m == 7 else None))
                wfree[q] = s_t.n
            for dm in range(DM):
                q = dm % NWB
                p.step("gpsimd", (lambda q=q, dm=dm: lambda e: e.dma_start(
                    out=wbuf[q], in_=d_ffnT[dm * 128:(dm + 1) * 128, :]))(),
                       waits=[(s_t, wfree[q])], inc=(s_wc[q], 16))
                wcmark = s_wc[q].n
                for m in range(8):
                    p.step("tensor",
                           (lambda q=q, dm=dm, m=m, par=par: lambda t: t.matmul(
                               ps_out[:, m * 128:(m + 1) * 128], wbuf[q][:, m * 128:(m + 1) * 128],
                               xn_fm[par][:, dm * T:(dm + 1) * T],
                               start=False, stop=False))(),
                           waits=[(s_wc[q], wcmark)],
                           inc=((s_t, 1) if m == 7 else None))
                wfree[q] = s_t.n
            xn_free_t[par] = s_t.n
            for m in range(8):
                p.step("tensor",
                       (lambda m=m, par=par: lambda t: t.matmul(
                           ps_out[:, m * 128:(m + 1) * 128], disha_sb[:, m * 128:(m + 1) * 128],
                           pooled_sb[par][:],
                           start=False, stop=(m % 4 == 3)))(),
                       inc=((s_t, 1) if m == 7 else None))
            out_mm = s_t.n
            y_free_t = out_mm
            pool_free_t[par] = out_mm
            cout[tau] = out_mm

        def phase_C_evac(tau):
            out_mm = cout.pop(tau)
            p.step("vector", lambda v: v.tensor_copy(fmout[:], ps_out[:]),
                   waits=[(s_t, out_mm)], inc=(s_v, 1))
            cev[tau] = s_v.n
            outps_free_v[0] = cev[tau]

        def phase_C_rest(tau, marks):
            nonlocal store_done
            par = tau % 2
            t0 = tau * T
            ev = cev.pop(tau)
            for m in range(8):
                bi = m % 2
                te, tm = tr_last[bi]
                tw = [(s_v, ev)] + ([(s_s, tm)] if te == "s" else [(s_v, tm)])
                p.step("tensor",
                       (lambda m=m, bi=bi: lambda t: t.transpose(
                           ps_tr[bi][:, 0:128], fmout[:, m * 128:(m + 1) * 128], ident))(),
                       waits=tw, inc=(s_t, 1))
                trd = s_t.n
                p.step("vector",
                       (lambda m=m, bi=bi, par3=tau % 3: lambda v: v.tensor_tensor(
                           out_tok[:, m * 128:(m + 1) * 128], ps_tr[bi][0:128, 0:128],
                           xstage[par3][:, m * 128:(m + 1) * 128], OP.add))(),
                       waits=[(s_t, trd), (s_x, marks["xw"]), (s_st, store_done)],
                       inc=(s_v, 1))
                tr_last[bi] = ("v", s_v.n)
            adds_done = s_v.n
            x_free_v[tau % 3] = adds_done
            p.step("sync", (lambda t0=t0: lambda e: e.dma_start(
                out=d_out[t0:t0 + T, :], in_=out_tok[:]))(),
                   waits=[(s_v, adds_done)], inc=(s_st, 16))
            store_done = s_st.n


        marks = {}
        y3s = {}
        st0 = phase_A_head(0)
        phase_A_pool(st0)
        for e in range(4):
            phase_A_eg_mm(0, e, st0)
            phase_A_eg_ev(0, e, st0)
        marks[0] = phase_A_tail(0, st0)
        for tau in range(NT):
            pieces = []
            end_pieces = []
            if tau >= 1:
                phase_C_mm(tau - 1, y3s[tau - 1])
            if tau + 1 < NT:
                cell = {}
                pieces.append((lambda cell=cell, t2=tau + 1: (lambda: cell.__setitem__("st", phase_A_head(t2))))())
                pieces.append((lambda cell=cell, t2=tau + 1: (lambda: phase_A_eg_mm(t2, 0, cell["st"])))())
            if tau >= 1:
                pieces.append((lambda t1=tau - 1: (lambda: phase_C_evac(t1)))())
            if tau + 1 < NT:
                pieces.append((lambda cell=cell: (lambda: phase_A_pool(cell["st"])))())
                pieces.append((lambda cell=cell, t2=tau + 1: (lambda: phase_A_eg_ev(t2, 0, cell["st"])))())
                pieces.append((lambda cell=cell, t2=tau + 1: (lambda: phase_A_eg_mm(t2, 1, cell["st"])))())
            if tau >= 1:
                pieces.append((lambda t1=tau - 1: (lambda: phase_C_rest(t1, marks[t1])))())
            if tau + 1 < NT:
                for e in range(1, 4):
                    pieces.append((lambda e=e, cell=cell, t2=tau + 1: (lambda: phase_A_eg_ev(t2, e, cell["st"])))())
                    if e + 1 < 4:
                        pieces.append((lambda e=e, cell=cell, t2=tau + 1: (lambda: phase_A_eg_mm(t2, e + 1, cell["st"])))())
                pieces.append((lambda cell=cell, t2=tau + 1: (lambda: marks.__setitem__(t2, phase_A_tail(t2, cell["st"])))) ())
                end_pieces.append((lambda t2=tau + 1: (lambda: phase_B_prol(t2, marks[t2])))())

            TAKES = [2, 2, 2, 3, 2, 1, 1, 0, 99]

            def inter(k, pieces=pieces, end_pieces=end_pieces, TAKES=TAKES):
                if k == 8:
                    while pieces:
                        pieces.pop(0)()
                    while end_pieces:
                        end_pieces.pop(0)()
                    return
                for _ in range(TAKES[k]):
                    if pieces:
                        pieces.pop(0)()

            y3s[tau] = phase_B(tau, marks[tau], inter)
        phase_C_mm(NT - 1, y3s[NT - 1])
        phase_C_evac(NT - 1)
        phase_C_rest(NT - 1, marks[NT - 1])

        final_store = s_st.n
        p.step("sync", lambda e: e.nop(), waits=[(s_st, final_store)])

        with nc.Block() as block:
            @block.sync
            def _(sync):
                p.emit("sync", sync)

            @block.vector
            def _(vector):
                p.emit("vector", vector)

            @block.scalar
            def _(scalar):
                p.emit("scalar", scalar)

            @block.tensor
            def _(tensor):
                p.emit("tensor", tensor)

            @block.gpsimd
            def _(gpsimd):
                p.emit("gpsimd", gpsimd)

    return nc


def _host_layout(inputs):
    W_in = np.ascontiguousarray(inputs["L0_attn_W_in"])
    W_out = np.ascontiguousarray(inputs["L0_attn_W_out"])
    ffn_w = np.ascontiguousarray(inputs["L0_ffn_weight"])
    winT = np.ascontiguousarray(W_in.T.astype(np.float32))
    woutT = np.ascontiguousarray(W_out.T.astype(np.float32))
    ffnT = np.ascontiguousarray(ffn_w.T.astype(np.float32))
    disha = np.ascontiguousarray(inputs["L0_ffn_disha"].astype(np.float32))

    W_x = inputs["L0_attn_W_x"]
    wx = np.ascontiguousarray(
        W_x.T.reshape(DC, 128, 33).transpose(1, 0, 2).reshape(128, DC * 33).astype(np.float32))

    def chanv(a):
        return a.reshape(DC, 128).T

    chan = np.zeros((128, 176), np.float32)
    chan[:, 0:16] = chanv(inputs["L0_attn_W_dt"][:, 0])
    chan[:, 16:32] = chanv(inputs["L0_attn_b_dt"])
    chan[:, 32:48] = chanv(inputs["L0_attn_conv_b"])
    chan[:, 48:64] = chanv(inputs["L0_attn_D_param"])
    chan[:, 64:80] = chanv(inputs["L0_attn_ln_g"])
    chan[:, 80:96] = chanv(inputs["L0_attn_ln_b"])
    cw = inputs["L0_attn_conv_w"][:, 0, :]
    chan[:, 96:160] = cw.reshape(DC, 128, 4).transpose(1, 0, 2).reshape(128, 64)
    chan[:, 160:176] = np.broadcast_to(inputs["L0_attn_A_log"], (128, 16))

    cons = np.zeros((128, 195), np.float32)
    cons[:, 0:64] = np.tile(np.eye(64, dtype=np.float32), (2, 1))
    cons[:, 64] = 1.0
    cons[:, 65:193] = np.eye(128, dtype=np.float32)
    cons[:, 193] = LN_EPS
    cons[:, 194] = 1.0 / DIN

    return dict(winT=winT, woutT=woutT, ffnT=ffnT, disha=disha, wx=wx,
                chan=chan, cons=cons)


def kernel(**inputs):
    shared = _host_layout(inputs)
    x = np.ascontiguousarray(inputs["L0_attn_x"].astype(np.float32))
    nc = build_program()
    in_maps = []
    for b in range(8):
        m = dict(shared)
        m["x"] = np.ascontiguousarray(x[b])
        in_maps.append(m)
    res = run_bass_kernel_spmd(nc, in_maps, core_ids=list(range(8)))
    out = np.stack([r["out"] for r in res.results], axis=0)
    return out.astype(np.float32)
